# version 02
# baseline (speedup 1.0000x reference)
import numpy as np
import ml_dtypes
from contextlib import ExitStack
import concourse.bass as bass
import concourse.mybir as mybir
from concourse.bass_utils import run_bass_kernel_spmd

F32 = mybir.dt.float32
BF16 = mybir.dt.bfloat16
I32 = mybir.dt.int32
AF = mybir.ActivationFunctionType
ALU = mybir.AluOpType


class Sem:
    def __init__(self, es, nc, name):
        self.h = es.enter_context(nc.semaphore(name))
        self.n = 0
        self.name = name

    def inc(self, instr, k=1):
        instr.then_inc(self.h, k)
        self.n += k
        return self.n


class K:
    ENGS = ("tensor", "vector", "scalar", "gpsimd", "sync")

    def __init__(self, nc, es):
        self.nc = nc
        self.es = es
        self.sem = {e: Sem(es, nc, "s_" + e) for e in ("tensor", "vector", "scalar", "gpsimd")}
        self.cc = Sem(es, nc, "s_cc")
        self.chans = {}
        self.state = {}
        self.observed = {e: {} for e in self.ENGS}
        self.block = None

    def chan(self, name):
        if name not in self.chans:
            self.chans[name] = Sem(self.es, self.nc, "ch_" + name)
        return self.chans[name]

    def _wait(self, engobj, eng, sem, val):
        ob = self.observed[eng]
        if ob.get(sem.name, -1) >= val:
            return
        engobj.wait_ge(sem.h, val)
        ob[sem.name] = val

    def op(self, eng, reads, writes, emit, chan=None, inc=1):
        sem = chan if chan is not None else self.sem[eng]
        k = self

        def body(engobj):
            for t in reads:
                st = k.state.get(t)
                if st and st["w"]:
                    k._wait(engobj, eng, *st["w"])
            for t in writes:
                st = k.state.get(t)
                if st:
                    if st["w"]:
                        k._wait(engobj, eng, *st["w"])
                    for s_, v in list(st["r"].items()):
                        k._wait(engobj, eng, s_, v)
            if chan is not None:
                k._wait(engobj, eng, chan, chan.n)
            last = emit(engobj)
            sem.inc(last, 16 if chan is not None else inc)

        getattr(self.block, eng)(body)
        mark = (sem, sem.n)
        for t in writes:
            self.state[t] = {"w": mark, "r": {}}
        for t in reads:
            st = self.state.setdefault(t, {"w": None, "r": {}})
            st["r"][mark[0]] = mark[1]
        return mark

    def wait_all(self, eng, resources):
        def body(engobj):
            for t in resources:
                st = self.state.get(t)
                if not st:
                    continue
                if st["w"]:
                    self._wait(engobj, eng, *st["w"])
                for s_, v in list(st["r"].items()):
                    self._wait(engobj, eng, s_, v)
        getattr(self.block, eng)(body)


T = 2048
NT = 4
LCH = 4
_ES = []


class NS:
    pass


def build(dbg_names=(), phases=("gla", "fprep", "ffn1")):
    nc = bass.Bass("TRN2", target_bir_lowering=False, debug=False)
    es = ExitStack()
    _ES.append(es)
    k = K(nc, es)
    n = NS()
    n.nc, n.k = nc, k

    din = lambda nm, sh, dt=F32: nc.dram_tensor(nm, sh, dt, kind="ExternalInput").ap()
    dout = lambda nm, sh, dt=F32: nc.dram_tensor(nm, sh, dt, kind="ExternalOutput").ap()
    sb = lambda nm, sh, dt=BF16: es.enter_context(nc.sbuf_tensor(nm, sh, dt))

    n.x_in = din("x", [1024, T])
    n.rolef_in = din("rolef", [128, 1])
    n.aW = {nm: din("aW" + nm, [1024, 1024]) for nm in "qkvo"}
    n.aWf = din("aWf", [1024, 16])
    n.adelta = din("adelta", [16, 1])
    n.anw = din("anw", [128, 8])
    n.fwfcT = din("fwfcT", [1024, 4096])
    n.fwprojT = din("fwprojT", [4096, 1024])
    n.fuT = din("fuT", [16, 4096])
    n.fv = din("fv", [16, 1024])
    n.ffb = din("ffb", [16, 16])
    n.fgate = din("fgate", [1, 1])
    n.rolei_in = din("rolei", [1, 1], I32)
    n.rowsel_in = din("rowsel", [128, 1])
    n.bln1g = din("bln1g", [128, 8])
    n.bln1b = din("bln1b", [128, 8])
    n.bln2g = din("bln2g", [128, 8])
    n.bln2b = din("bln2b", [128, 8])
    n.bqU = din("bqU", [1024, 1024])
    n.bkU = din("bkU", [1024, 1024])
    n.bvU = din("bvU", [1024, 1024])
    n.bqVb = din("bqVb", [128, 512])
    n.bkVb = din("bkVb", [128, 512])
    n.bvVb = din("bvVb", [128, 520])
    n.boutU = din("boutU", [1024, 768])
    n.boutV = din("boutV", [768, 1024])
    n.boutb = din("boutb", [128, 8])
    n.bfc1U = din("bfc1U", [1024, 512])
    n.bfc1V = din("bfc1V", [512, 4096])
    n.bfc1b = din("bfc1b", [128, 32])
    n.bfc2U = din("bfc2U", [4096, 512])
    n.bfc2V = din("bfc2V", [512, 1024])
    n.bfc2b = din("bfc2b", [128, 8])
    n.x_out = dout("xout", [1024, T])

    n.xfp = nc.dram_tensor("xfp", [1024, T], F32)
    n.weff_fc = nc.dram_tensor("weff_fc", [1024, 4096], BF16)
    n.weff_pj = nc.dram_tensor("weff_pj", [4096, 1024], BF16)
    n.gla_in = nc.dram_tensor("gla_in", [1040, 64], F32)
    n.gla_out = nc.dram_tensor("gla_out", [2, 1040, 64], F32)
    n.pq_in = nc.dram_tensor("pq_in", [1024, T], BF16)
    n.pk_in = nc.dram_tensor("pk_in", [1024, T], BF16)
    n.pv_in = nc.dram_tensor("pv_in", [1024, T], BF16)
    n.pq_out = nc.dram_tensor("pq_out", [2, 1024, T], BF16)
    n.pk_out = nc.dram_tensor("pk_out", [2, 1024, T], BF16)
    n.pv_out = nc.dram_tensor("pv_out", [2, 1024, T], BF16)
    n.rs_in = [nc.dram_tensor("rs_in%d" % j, [2, 16, 65, 512], BF16) for j in range(4)]
    n.rs_out = [nc.dram_tensor("rs_out%d" % j, [16, 65, 512], BF16) for j in range(4)]

    n.XBF = sb("XBF", [128, 8 * T + 256])
    n.XH = sb("XH", [128, 8 * 512])
    n.QS = sb("QS", [128, 8 * T])
    n.OT = sb("OT", [128, 8 * T])
    n.KD = sb("KD", [128, 8 * 512])
    n.VTOK = sb("VTOK", [128, 4 * 1024])
    n.KDTOK = sb("KDTOK", [128, 512])
    n.AM = sb("AM", [128, 2 * 512])
    n.W0 = sb("W0", [128, 8 * 1024])
    n.W1 = sb("W1", [128, 8 * 1024])
    n.WF = sb("WF", [128, 8 * 16])
    n.SST = sb("SST", [128, 8 * 64], F32)
    n.SPREB = sb("SPREB", [128, 4 * 64])
    n.USB = sb("USB", [128, 4 * 64], F32)
    n.TPOS = sb("TPOS", [16, 512], F32)
    n.CUM = sb("CUM", [16, 512], F32)
    n.EQ = sb("EQ", [16, 512])
    n.EK = sb("EK", [16, 512])
    n.CLALL = sb("CLALL", [16, 16], F32)
    n.NEGCL = sb("NEGCL", [16, 16], F32)
    n.DD = sb("DD", [16, 16], F32)
    n.DDB = sb("DDB", [128, 8 * 4], F32)
    n.INCL = sb("INCL", [16, 16], F32)
    n.PREFE = sb("PREFE", [16, 16], F32)
    n.PREFB = sb("PREFB", [128, 8 * 16], F32)
    n.ONESB = sb("ONESB", [128, 1])
    n.ONES16 = sb("ONES16", [16, 128], F32)
    n.ONESF = sb("ONESF", [128, 128], F32)
    n.SELB = sb("SELB", [16, 8 * 128])
    n.SELF_ = sb("SELF", [16, 8 * 128], F32)
    n.TRIL = sb("TRIL", [128, 128])
    n.IDENT = sb("IDENT", [128, 128])
    n.NDELTA = sb("NDELTA", [16, 1], F32)
    n.NW = sb("NW", [128, 8], F32)
    n.ROLEF = sb("ROLEF", [128, 1], F32)
    n.ROWA = sb("ROWA", [1, T], F32)
    n.ROWB = sb("ROWB", [1, T], F32)
    n.SCR1 = sb("SCR1", [1, 512], F32)
    n.SCR2 = sb("SCR2", [1, 512], F32)
    n.TMP32 = sb("TMP32", [128, 512], F32)
    n.CLS = sb("CLS", [16, 1], F32)
    n.CP16 = sb("CP16", [16, 64], F32)
    n.PST = sb("PST", [128, 8 * 64], F32)
    n.PCB = sb("PCB", [128, 2 * 64])
    n.XT32A = sb("XT32A", [128, 512], F32)
    n.XT32B = sb("XT32B", [128, 512], F32)
    n.XO32A = sb("XO32A", [128, 512], F32)
    n.XO32B = sb("XO32B", [128, 512], F32)
    n.GATEB = sb("GATEB", [128, 1], F32)
    n.GATES = sb("GATES", [1, 1], F32)
    n.ROLEI = sb("ROLEI", [1, 1], I32)
    n.ROWSEL = sb("ROWSEL", [128, 1], F32)
    n.LN1G = sb("LN1G", [128, 8], F32)
    n.LN1B = sb("LN1B", [128, 8], F32)
    n.LN2G = sb("LN2G", [128, 8], F32)
    n.LN2B = sb("LN2B", [128, 8], F32)
    n.VVB = sb("VVB", [128, 520])
    n.QVB = n.AM[:, 0:512]
    n.KVB = n.AM[:, 512:1024]
    n.SUMS = n.AM[0:16, 0:512]
    n.SUMSF = n.TMP32[0:16, 0:512]
    n.OUTB = sb("OUTB", [128, 8], F32)
    n.FC1B = sb("FC1B", [128, 32], F32)
    n.FC2B = sb("FC2B", [128, 8], F32)

    n.FFB = sb("FFB", [16, 16])
    n.WM = n.QS[:, 4096:12288]
    n.FUTS = n.QS[0:16, 0:4096]
    n.FT1 = n.QS[0:16, 4096:8192]
    n.FVB = n.OT[0:16, 0:1024]

    PSN = ["P0", "P1", "P2", "P3", "P4", "P6", "P7"]
    n.PS = {nm: es.enter_context(nc.psum_tensor(nm, [128, 512], F32)) for nm in PSN}
    n.P0, n.P1, n.P2, n.P3, n.P4, n.P6, n.P7 = (n.PS[nm] for nm in PSN)
    n.P5T = es.enter_context(nc.psum_tensor("P5T", [128, 1024], BF16))

    k.block = es.enter_context(nc.Block())
    with nc.named_scope("init"):
        init(n)
    if "gla" in phases:
        with nc.named_scope("gla"):
            gla(n)
    if "fprep" in phases:
        with nc.named_scope("fprep"):
            fprep(n)
    if "ffn1" in phases:
        last = "ffn1" == [p for p in ("gla", "ffn1", "attnb", "ffn2") if p in phases][-1]
        with nc.named_scope("ffn1"):
            ffn(n, n.xfp.ap(), "xfp", n.x_out if last else n.xfp.ap(), "x_out" if last else "xfp", tag="f1")
    if "attnb" in phases or "attnbC" in phases:
        last = "attnb" in [p for p in ("gla", "ffn1", "attnb", "ffn2") if p in phases][-1:] or "attnbC" in phases
        attnb(n, n.x_out if last else n.xfp.ap(), "x_out" if last else "xfp",
              stop="cache" if "attnbC" in phases else None)
    if "ffn2" in phases:
        with nc.named_scope("ffn2"):
            ffn(n, n.xfp.ap(), "xfp", n.x_out, "x_out", tag="f2")

    emit_debug(n, dbg_names)
    fin(n)
    return nc


def init(n):
    k, op = n.k, n.k.op
    xv = n.x_in.rearrange("(c p) t -> c p t", c=8).transpose([1, 0, 2])
    op("gpsimd", ["x_in"], ["XBF"],
       lambda g: g.dma_start(out=n.XBF[:, 0:8 * T].rearrange("p (c t) -> p c t", c=8), in_=xv),
       chan=k.chan("xb"))
    op("gpsimd", [], ["NDELTA"], lambda g: g.dma_start(out=n.NDELTA[:], in_=n.adelta[:, :]), chan=k.chan("cn"))
    op("gpsimd", [], ["NW"], lambda g: g.dma_start(out=n.NW[:], in_=n.anw[:, :]), chan=k.chan("cn2"))
    op("gpsimd", [], ["ROLEF"], lambda g: g.dma_start(out=n.ROLEF[:], in_=n.rolef_in[:, :]), chan=k.chan("cn3"))
    pass
    op("gpsimd", [], ["ONESB"], lambda g: g.memset(n.ONESB[:], 1.0))
    op("gpsimd", [], ["ONES16"], lambda g: g.memset(n.ONES16[:], 1.0))
    op("gpsimd", [], ["ONESF"], lambda g: g.memset(n.ONESF[:], 1.0))
    op("gpsimd", [], ["SST"], lambda g: g.memset(n.SST[:], 0.0))
    op("gpsimd", [], ["IDENT0"], lambda g: g.memset(n.IDENT[:], 0.0))
    op("gpsimd", [], ["TRIL0"], lambda g: g.memset(n.TRIL[:], 1.0))
    op("gpsimd", [], ["SELB0"], lambda g: g.memset(n.SELB[:], 1.0))
    op("gpsimd", [], ["SELF0"], lambda g: g.memset(n.SELF_[:], 1.0))
    op("gpsimd", ["IDENT0"], ["IDENT"],
       lambda g: g.affine_select(out=n.IDENT[:], in_=n.IDENT[:], compare_op=ALU.not_equal,
                                 fill=1.0, base=0, pattern=[[-1, 128]], channel_multiplier=1))
    op("gpsimd", ["TRIL0"], ["TRIL"],
       lambda g: g.affine_select(out=n.TRIL[:], in_=n.TRIL[:], compare_op=ALU.is_ge,
                                 fill=0.0, base=0, pattern=[[1, 128]], channel_multiplier=-1))
    op("gpsimd", ["SELB0"], ["SELB"],
       lambda g: g.affine_select(out=n.SELB[:], in_=n.SELB[:], compare_op=ALU.is_equal,
                                 fill=0.0, base=0, pattern=[[-1, 16], [0, 64]], channel_multiplier=1))
    op("gpsimd", ["SELF0"], ["SELF"],
       lambda g: g.affine_select(out=n.SELF_[:], in_=n.SELF_[:], compare_op=ALU.is_equal,
                                 fill=0.0, base=0, pattern=[[-1, 16], [0, 64]], channel_multiplier=1))
    op("vector", ["NDELTA"], ["NDELTA"], lambda v: v.tensor_scalar_mul(n.NDELTA[:], n.NDELTA[:], -1.0))
    op("gpsimd", [], ["GATES"], lambda g: g.dma_start(out=n.GATES[:, :], in_=n.fgate[:, :]), chan=k.chan("cn6"))
    op("gpsimd", [], ["FFB"], lambda g: g.dma_start(out=n.FFB[:, :], in_=n.ffb[:, :]), chan=k.chan("cn5"))
    pass
    op("tensor", ["ONESF", "GATES"], ["P0"],
       lambda t: t.matmul(n.P0[:, 0:1], n.ONESF[0:1, :], n.GATES[0:1, 0:1], start=True, stop=True))
    op("vector", ["P0"], ["GATEB"], lambda v: v.tensor_copy(n.GATEB[:, :], n.P0[:, 0:1]))
    op("gpsimd", [], ["ROLEI"], lambda g: g.dma_start(out=n.ROLEI[:, :], in_=n.rolei_in[:, :]), chan=k.chan("cn8"))
    op("gpsimd", [], ["ROWSEL"], lambda g: g.dma_start(out=n.ROWSEL[:, :], in_=n.rowsel_in[:, :]), chan=k.chan("cn8"))
    for nm, dst, res in (("bln1g", n.LN1G, "LN1G"), ("bln1b", n.LN1B, "LN1B"),
                         ("bln2g", n.LN2G, "LN2G"), ("bln2b", n.LN2B, "LN2B"),
                         ("bvVb", n.VVB, "VVB"),
                         ("boutb", n.OUTB, "OUTB"), ("bfc1b", n.FC1B, "FC1B"),
                         ("bfc2b", n.FC2B, "FC2B")):
        op("gpsimd", [], [res], lambda g, nm=nm, dst=dst: g.dma_start(
            out=dst[:, :], in_=getattr(n, nm)[:, :]), chan=k.chan("cn9"))


def ln_full(n, eps=1e-5, srct=None, srcn="XBF"):
    k, op = n.k, n.k.op
    if srct is None:
        srct = n.XBF
    for tt in range(NT):
        t0 = tt * 512

        def sq(v, t0=t0):
            for c in range(8):
                i = v.tensor_mul(n.XH[:, c * 512:(c + 1) * 512],
                                 srct[:, c * T + t0:c * T + t0 + 512],
                                 srct[:, c * T + t0:c * T + t0 + 512])
            return i
        op("vector", [srcn], ["XH"], sq)

        def sums(t, t0=t0):
            for c in range(8):
                t.matmul(n.P0[0:1, :], n.ONESB[:], srct[:, c * T + t0:c * T + t0 + 512],
                         start=(c == 0), stop=(c == 7))
            for c in range(8):
                i = t.matmul(n.P1[0:1, :], n.ONESB[:], n.XH[:, c * 512:(c + 1) * 512],
                             start=(c == 0), stop=(c == 7))
            return i
        op("tensor", [srcn, "XH", "ONESB"], ["P0", "P1"], sums)
        op("vector", ["P0"], ["ROWA"],
           lambda v, t0=t0: v.tensor_scalar_mul(n.ROWA[0:1, t0:t0 + 512], n.P0[0:1, :], 1.0 / 1024.0))
        op("vector", ["P1"], ["SCR2"],
           lambda v: v.tensor_scalar_mul(n.SCR2[0:1, :], n.P1[0:1, :], 1.0 / 1024.0))
        op("vector", ["ROWA"], ["SCR1"],
           lambda v, t0=t0: v.tensor_mul(n.SCR1[0:1, :], n.ROWA[0:1, t0:t0 + 512], n.ROWA[0:1, t0:t0 + 512]))
        op("vector", ["SCR1", "SCR2"], ["SCR1"],
           lambda v: v.tensor_sub(n.SCR1[0:1, :], n.SCR2[0:1, :], n.SCR1[0:1, :]))
        op("vector", ["SCR1"], ["SCR1"],
           lambda v: v.tensor_scalar_add(n.SCR1[0:1, :], n.SCR1[0:1, :], eps))
        op("vector", ["SCR1"], ["SCR2"], lambda v: v.reciprocal(n.SCR2[0:1, :], n.SCR1[0:1, :]))
        op("scalar", ["SCR2"], ["ROWB"],
           lambda s, t0=t0: s.activation(out=n.ROWB[0:1, t0:t0 + 512], in_=n.SCR2[0:1, :], func=AF.Sqrt))


def ln_apply(n, t0, srct=None, srcn="XBF", gt=None, gn=None, bt=None, bn=None):
    k, op = n.k, n.k.op
    if srct is None:
        srct = n.XBF

    def bc(t):
        t.matmul(n.P6[:, :], n.ONESF[0:1, :], n.ROWB[0:1, t0:t0 + 512], start=True, stop=True)
        return t.matmul(n.P7[:, :], n.ONESF[0:1, :], n.ROWA[0:1, t0:t0 + 512], start=True, stop=True)
    op("tensor", ["ROWA", "ROWB", "ONESF"], ["P6", "P7"], bc)
    for c in range(8):
        op("vector", [srcn, "P7"], ["TMP32"],
           lambda v, c=c: v.tensor_sub(n.TMP32[:], srct[:, c * T + t0:c * T + t0 + 512], n.P7[:, :]))
        op("vector", ["TMP32", "P6"], ["XH"],
           lambda v, c=c: v.tensor_mul(n.XH[:, c * 512:(c + 1) * 512], n.TMP32[:], n.P6[:, :]))
        if gt is not None:
            op("vector", ["XH", gn, bn], ["XH"],
               lambda v, c=c: v.tensor_scalar(n.XH[:, c * 512:(c + 1) * 512],
                                              n.XH[:, c * 512:(c + 1) * 512],
                                              gt[:, c:c + 1], bt[:, c:c + 1],
                                              ALU.mult, ALU.add))


def res_write(n, src_dram, src_name, dst_dram, dst_name, fc, t0, idx, bank, bankname):
    k, op = n.k, n.k.op
    r = idx % 2
    XT = n.XT32A if r == 0 else n.XT32B
    XO = n.XO32A if r == 0 else n.XO32B
    xtn = "XT32%s" % ("A" if r == 0 else "B")
    xon = "XO32%s" % ("A" if r == 0 else "B")
    op("sync", [src_name], [xtn],
       lambda s: s.dma_start(out=XT[:], in_=src_dram[fc * 128:(fc + 1) * 128, t0:t0 + 512]),
       chan=k.chan("xt%d" % r))
    op("vector", [bankname, xtn], [xon],
       lambda v: v.tensor_add(XO[:], bank[:, :], XT[:]))
    op("vector", [xon], ["XBF"],
       lambda v: v.tensor_copy(n.XBF[:, fc * T + t0:fc * T + t0 + 512], XO[:]))
    op("sync", [xon], [dst_name],
       lambda s: s.dma_start(out=dst_dram[fc * 128:(fc + 1) * 128, t0:t0 + 512], in_=XO[:]),
       chan=k.chan("xo%d" % r))


def gla(n):
    k, op = n.k, n.k.op
    nc = n.nc
    QS, OT, KD, XH, XBF = n.QS, n.OT, n.KD, n.XH, n.XBF
    P0, P1, P2, P3, P4, P6, P7, P5T = n.P0, n.P1, n.P2, n.P3, n.P4, n.P6, n.P7, n.P5T

    ln_full(n, 1e-5)

    for g in range(NT):
        t0 = g * 512
        ln_apply(n, t0)

        op("gpsimd", ["aWq"], ["W0"],
           lambda g2: g2.dma_start(out=n.W0.rearrange("p (c f) -> p c f", c=8),
                                   in_=n.aW["q"].rearrange("(c p) f -> c p f", c=8).transpose([1, 0, 2])),
           chan=k.chan("w0"))
        op("gpsimd", ["aWk"], ["W1"],
           lambda g2: g2.dma_start(out=n.W1.rearrange("p (c f) -> p c f", c=8),
                                   in_=n.aW["k"].rearrange("(c p) f -> c p f", c=8).transpose([1, 0, 2])),
           chan=k.chan("w1"))
        if g == 0:
            op("gpsimd", ["aWf"], ["WF"],
               lambda g2: g2.dma_start(out=n.WF.rearrange("p (c f) -> p c f", c=8),
                                       in_=n.aWf.rearrange("(c p) f -> c p f", c=8).transpose([1, 0, 2])),
               chan=k.chan("wf"))

        def proj(W, Wname, dst_fn, dstname):
            for fc in range(8):
                bn = ["P0", "P1", "P2", "P3"][fc % 4]
                bank = n.PS[bn]

                def mmf(t, fc=fc, bank=bank, W=W):
                    for dc in range(8):
                        i = t.matmul(bank[:, :],
                                     W[:, dc * 1024 + fc * 128:dc * 1024 + fc * 128 + 128],
                                     XH[:, dc * 512:(dc + 1) * 512],
                                     start=(dc == 0), stop=(dc == 7))
                    return i
                op("tensor", [Wname, "XH"], [bn], mmf)
                op("scalar", [bn], [dstname],
                   lambda s, fc=fc, bank=bank: s.activation(out=dst_fn(fc), in_=bank[:, :], func=AF.Silu))

        proj(n.W0, "W0", lambda fc: QS[:, fc * T + t0:fc * T + t0 + 512], "QS")
        op("gpsimd", ["aWv"], ["W0"],
           lambda g2: g2.dma_start(out=n.W0.rearrange("p (c f) -> p c f", c=8),
                                   in_=n.aW["v"].rearrange("(c p) f -> c p f", c=8).transpose([1, 0, 2])),
           chan=k.chan("w0"))
        proj(n.W1, "W1", lambda fc: KD[:, fc * 512:(fc + 1) * 512], "KD")

        def fmm(t):
            for dc in range(8):
                i = t.matmul(P6[0:16, :], n.WF[:, dc * 16:(dc + 1) * 16],
                             XH[:, dc * 512:(dc + 1) * 512], start=(dc == 0), stop=(dc == 7))
            return i
        op("tensor", ["WF", "XH"], ["P6"], fmm)
        op("scalar", ["P6", "NDELTA"], ["TPOS"],
           lambda s: s.activation(out=n.TPOS[:, :], in_=P6[0:16, :], func=AF.Exp, scale=-1.0, bias=n.NDELTA[:, :]))
        op("vector", ["TPOS"], ["TPOS"], lambda v: v.tensor_scalar_add(n.TPOS[:, :], n.TPOS[:, :], 1.0))
        op("scalar", ["TPOS"], ["TPOS"], lambda s: s.activation(out=n.TPOS[:, :], in_=n.TPOS[:, :], func=AF.Ln))

        for lc in range(LCH):
            co = lc * 128

            def vmm1(t, co=co):
                for dc in range(8):
                    i = t.matmul(P4[:, :], XH[:, dc * 512 + co:dc * 512 + co + 128],
                                 n.W0[:, dc * 1024:dc * 1024 + 512], start=(dc == 0), stop=(dc == 7))
                return i
            op("tensor", ["W0", "XH"], ["P4"], vmm1)
            op("vector", ["P4"], ["VTOK.c%d" % lc],
               lambda v, lc=lc: v.tensor_copy(n.VTOK[:, lc * 1024:lc * 1024 + 512], P4[:, :]))

            def vmm2(t, co=co):
                for dc in range(8):
                    i = t.matmul(P4[:, :], XH[:, dc * 512 + co:dc * 512 + co + 128],
                                 n.W0[:, dc * 1024 + 512:dc * 1024 + 1024], start=(dc == 0), stop=(dc == 7))
                return i
            op("tensor", ["W0", "XH"], ["P4"], vmm2)
            op("vector", ["P4"], ["VTOK.c%d" % lc],
               lambda v, lc=lc: v.tensor_copy(n.VTOK[:, lc * 1024 + 512:lc * 1024 + 1024], P4[:, :]))

        for lc in range(LCH):
            op("vector", ["TPOS", "ONES16"], ["CUM"],
               lambda v, lc=lc: v.tensor_tensor_scan(n.CUM[:, lc * 128:(lc + 1) * 128],
                                                     n.TPOS[:, lc * 128:(lc + 1) * 128],
                                                     n.ONES16[:, :], 0.0, ALU.add, ALU.mult))
        op("vector", ["CUM"], ["CLALL"],
           lambda v, g=g: v.tensor_copy(n.CLALL[:, g * 4:(g + 1) * 4], n.CUM[:, 127::128]))
        op("vector", ["CLALL"], ["NEGCL"],
           lambda v, g=g: v.tensor_scalar_mul(n.NEGCL[:, g * 4:(g + 1) * 4], n.CLALL[:, g * 4:(g + 1) * 4], -1.0))
        op("scalar", ["CUM"], ["EQ"],
           lambda s: s.activation(out=n.EQ[:, :], in_=n.CUM[:, :], func=AF.Exp, scale=-1.0))
        op("scalar", ["CUM"], ["EK"],
           lambda s: s.activation(out=n.EK[:, :], in_=n.CUM[:, :], func=AF.Exp))
        op("scalar", ["NEGCL"], ["DD"],
           lambda s, g=g: s.activation(out=n.DD[:, g * 4:(g + 1) * 4], in_=n.NEGCL[:, g * 4:(g + 1) * 4], func=AF.Exp))

        def ddbmm(t, g=g):
            for pr in range(8):
                i = t.matmul(P7[:, pr * 4:(pr + 1) * 4], n.SELF_[:, pr * 128:(pr + 1) * 128],
                             n.DD[:, g * 4:(g + 1) * 4], start=True, stop=True)
            return i
        op("tensor", ["SELF", "DD"], ["P7"], ddbmm)
        op("vector", ["P7"], ["DDB"], lambda v: v.tensor_copy(n.DDB[:, :], P7[:, 0:32]))

        for pr in range(8):
            for lc in range(LCH):
                def ebc(t, pr=pr, lc=lc):
                    t.matmul(P4[:, 0:128], n.SELB[:, pr * 128:(pr + 1) * 128],
                             n.EQ[:, lc * 128:(lc + 1) * 128], start=True, stop=True)
                    return t.matmul(P4[:, 128:256], n.SELB[:, pr * 128:(pr + 1) * 128],
                                    n.EK[:, lc * 128:(lc + 1) * 128], start=True, stop=True)
                op("tensor", ["SELB", "EQ", "EK"], ["P4"], ebc)

                def qskd(v, pr=pr, lc=lc):
                    qsl = QS[:, pr * T + t0 + lc * 128: pr * T + t0 + lc * 128 + 128]
                    kdl = KD[:, pr * 512 + lc * 128: pr * 512 + lc * 128 + 128]
                    v.tensor_mul(qsl, qsl, P4[:, 0:128])
                    return v.tensor_mul(kdl, kdl, P4[:, 128:256])
                op("vector", ["P4", "QS", "KD"], ["QS", "KD"], qskd)

            def tps(t, pr=pr):
                for lc in range(LCH):
                    i = t.transpose(P5T[:, lc * 128:(lc + 1) * 128],
                                    KD[:, pr * 512 + lc * 128: pr * 512 + lc * 128 + 128], n.IDENT[:])
                return i
            op("tensor", ["KD", "IDENT"], ["P5T"], tps)
            op("vector", ["P5T"], ["KDTOK"], lambda v: v.tensor_copy(n.KDTOK[:, 0:512], P5T[:, 0:512]))

            for lc in range(LCH):
                def at(t, pr=pr, lc=lc):
                    qsl = QS[:, pr * T + t0 + lc * 128: pr * T + t0 + lc * 128 + 128]
                    kdl = KD[:, pr * 512 + lc * 128: pr * 512 + lc * 128 + 128]
                    t.matmul(P4[:, 256:384], kdl[0:64, :], qsl[0:64, :], start=True, stop=True)
                    return t.matmul(P7[:, 256:384], kdl[64:128, :], qsl[64:128, :], start=True, stop=True,
                                    tile_position=(64, 0))
                op("tensor", ["KD", "QS"], ["P4", "P7"], at)

                def msk(v, lc=lc):
                    v.tensor_mul(n.AM[:, lc * 128:(lc + 1) * 128], P4[:, 256:384], n.TRIL[:])
                    return v.tensor_mul(n.AM[:, 512 + lc * 128:512 + (lc + 1) * 128], P7[:, 256:384], n.TRIL[:])
                op("vector", ["P4", "P7", "TRIL"], ["AM"], msk)

                def ou(t, pr=pr, lc=lc):
                    vt = n.VTOK[:, lc * 1024 + pr * 128: lc * 1024 + pr * 128 + 128]
                    kt = n.KDTOK[:, lc * 128:(lc + 1) * 128]
                    t.matmul(P6[0:64, 0:128], vt[:, 0:64], n.AM[:, lc * 128:(lc + 1) * 128],
                             start=True, stop=True)
                    t.matmul(P6[64:128, 0:128], vt[:, 64:128], n.AM[:, 512 + lc * 128:512 + (lc + 1) * 128],
                             start=True, stop=True, tile_position=(0, 64))
                    t.matmul(P6[0:64, 256:320], kt[:, 0:64], vt[:, 0:64], start=True, stop=True)
                    return t.matmul(P6[64:128, 256:320], kt[:, 64:128], vt[:, 64:128],
                                    start=True, stop=True, tile_position=(0, 64))
                op("tensor", ["VTOK.c%d" % lc, "KDTOK", "AM"], ["P6"], ou)

                op("vector", ["P6"], ["OT"],
                   lambda v, pr=pr, lc=lc: v.tensor_copy(
                       OT[:, pr * T + t0 + lc * 128: pr * T + t0 + lc * 128 + 128], P6[:, 0:128]))
                op("vector", ["P6"], ["USB"],
                   lambda v, lc=lc: v.tensor_copy(n.USB[:, lc * 64:(lc + 1) * 64], P6[:, 256:320]))
                op("vector", ["SST"], ["SPREB"],
                   lambda v, pr=pr, lc=lc: v.tensor_copy(n.SPREB[:, lc * 64:(lc + 1) * 64],
                                                         n.SST[:, pr * 64:(pr + 1) * 64]))
                op("vector", ["SST", "USB"], ["SST"],
                   lambda v, pr=pr, lc=lc: v.tensor_add(n.SST[:, pr * 64:(pr + 1) * 64],
                                                        n.SST[:, pr * 64:(pr + 1) * 64],
                                                        n.USB[:, lc * 64:(lc + 1) * 64]))
                op("vector", ["SST", "DDB"], ["SST"],
                   lambda v, pr=pr, lc=lc: v.tensor_scalar(n.SST[:, pr * 64:(pr + 1) * 64],
                                                           n.SST[:, pr * 64:(pr + 1) * 64],
                                                           n.DDB[:, pr * 4 + lc:pr * 4 + lc + 1], None, ALU.mult))

            for lc in range(LCH):
                def oi(t, pr=pr, lc=lc):
                    sp = n.SPREB[:, lc * 64:(lc + 1) * 64]
                    qsl = QS[:, pr * T + t0 + lc * 128: pr * T + t0 + lc * 128 + 128]
                    t.matmul(P6[0:64, 128:256], sp[0:64, :], qsl[0:64, :], start=True, stop=True)
                    return t.matmul(P6[64:128, 128:256], sp[64:128, :], qsl[64:128, :], start=True, stop=True,
                                    tile_position=(64, 64))
                op("tensor", ["SPREB", "QS"], ["P6"], oi)
                op("vector", ["P6", "OT"], ["OT"],
                   lambda v, pr=pr, lc=lc: v.tensor_add(
                       OT[:, pr * T + t0 + lc * 128: pr * T + t0 + lc * 128 + 128],
                       OT[:, pr * T + t0 + lc * 128: pr * T + t0 + lc * 128 + 128], P6[:, 128:256]))

    op("vector", ["CLALL"], ["CLS"],
       lambda v: v.reduce_sum(n.CLS[:, :], n.CLALL[:, :], axis=mybir.AxisListType.X))
    op("scalar", ["CLS"], ["CLS"],
       lambda s: s.activation(out=n.CLS[:, :], in_=n.CLS[:, :], func=AF.Exp, scale=-1.0))
    for pr in range(8):
        op("gpsimd", ["SST"], ["gla_in"],
           lambda g2, pr=pr: g2.dma_start(out=n.gla_in.ap()[pr * 128:(pr + 1) * 128, :],
                                          in_=n.SST[:, pr * 64:(pr + 1) * 64]), chan=k.chan("ga"))
    op("vector", [], ["CP16"], lambda v: v.memset(n.CP16[:, :], 0.0))
    op("vector", ["CLS", "CP16"], ["CP16"], lambda v: v.tensor_copy(n.CP16[:, 0:1], n.CLS[:, :]))
    op("gpsimd", ["CP16"], ["gla_in"],
       lambda g2: g2.dma_start(out=n.gla_in.ap()[1024:1040, :], in_=n.CP16[:, :]), chan=k.chan("ga"))

    st_gla = k.state.get("gla_in")

    def coll_body(g2):
        if st_gla and st_gla["w"]:
            k._wait(g2, "gpsimd", *st_gla["w"])
        i = g2.collective_compute("AllGather", ALU.bypass,
                                  replica_groups=[[0, 1], [2, 3], [4, 5], [6, 7]],
                                  ins=[n.gla_in.ap().opt()], outs=[n.gla_out.ap().opt()])
        k.cc.inc(i, 1)
    k.block.gpsimd(coll_body)
    k.state["gla_out"] = {"w": (k.cc, k.cc.n), "r": {}}

    for pr in range(8):
        op("gpsimd", ["gla_out"], ["PST"],
           lambda g2, pr=pr: g2.dma_start(out=n.PST[:, pr * 64:(pr + 1) * 64],
                                          in_=n.gla_out.ap()[0, pr * 128:(pr + 1) * 128, :]), chan=k.chan("ps"))
    op("vector", ["PST", "ROLEF"], ["PST"],
       lambda v: v.tensor_scalar(n.PST[:, :], n.PST[:, :], n.ROLEF[:, :], None, ALU.mult))
    op("vector", ["CLALL", "ONES16"], ["INCL"],
       lambda v: v.tensor_tensor_scan(n.INCL[:, :], n.CLALL[:, :], n.ONES16[:, 0:16], 0.0, ALU.add, ALU.mult))
    op("vector", ["CLALL", "INCL"], ["INCL"],
       lambda v: v.tensor_sub(n.INCL[:, :], n.CLALL[:, :], n.INCL[:, :]))
    op("scalar", ["INCL"], ["PREFE"],
       lambda s: s.activation(out=n.PREFE[:, :], in_=n.INCL[:, :], func=AF.Exp))

    def prefbmm(t):
        for pr in range(8):
            i = t.matmul(P7[:, pr * 16:(pr + 1) * 16], n.SELF_[:, pr * 128:(pr + 1) * 128],
                         n.PREFE[:, :], start=True, stop=True)
        return i
    op("tensor", ["SELF", "PREFE"], ["P7"], prefbmm)
    op("vector", ["P7"], ["PREFB"], lambda v: v.tensor_copy(n.PREFB[:, :], P7[:, 0:128]))

    for pr in range(8):
        for ch in range(16):
            buf = ch % 2
            op("vector", ["PST", "PREFB"], ["PCB%d" % buf],
               lambda v, pr=pr, ch=ch, buf=buf: v.tensor_scalar(
                   n.PCB[:, buf * 64:(buf + 1) * 64], n.PST[:, pr * 64:(pr + 1) * 64],
                   n.PREFB[:, pr * 16 + ch:pr * 16 + ch + 1], None, ALU.mult))

            def cmm(t, pr=pr, ch=ch, buf=buf):
                sp = n.PCB[:, buf * 64:(buf + 1) * 64]
                qsl = QS[:, pr * T + ch * 128: pr * T + ch * 128 + 128]
                t.matmul(P6[0:64, 128:256], sp[0:64, :], qsl[0:64, :], start=True, stop=True)
                return t.matmul(P6[64:128, 128:256], sp[64:128, :], qsl[64:128, :], start=True, stop=True,
                                tile_position=(64, 64))
            op("tensor", ["PCB%d" % buf, "QS"], ["P6"], cmm)
            op("vector", ["P6", "OT"], ["OT"],
               lambda v, pr=pr, ch=ch: v.tensor_add(
                   OT[:, pr * T + ch * 128: pr * T + ch * 128 + 128],
                   OT[:, pr * T + ch * 128: pr * T + ch * 128 + 128], P6[:, 128:256]))

    for tt in range(NT):
        t0 = tt * 512

        def sq2(v, t0=t0):
            for pr in range(8):
                i = v.tensor_mul(XH[:, pr * 512:(pr + 1) * 512],
                                 OT[:, pr * T + t0:pr * T + t0 + 512],
                                 OT[:, pr * T + t0:pr * T + t0 + 512])
            return i
        op("vector", ["OT"], ["XH"], sq2)

        def sums2(t):
            for pr in range(8):
                i = t.matmul(P0[0:1, :], n.ONESB[:], XH[:, pr * 512:(pr + 1) * 512],
                             start=(pr == 0), stop=(pr == 7))
            return i
        op("tensor", ["XH", "ONESB"], ["P0"], sums2)
        op("vector", ["P0"], ["SCR1"], lambda v: v.tensor_scalar_mul(n.SCR1[0:1, :], P0[0:1, :], 1.0 / 1024.0))
        op("vector", ["SCR1"], ["SCR1"], lambda v: v.tensor_scalar_add(n.SCR1[0:1, :], n.SCR1[0:1, :], 1e-6))
        op("vector", ["SCR1"], ["SCR2"], lambda v: v.reciprocal(n.SCR2[0:1, :], n.SCR1[0:1, :]))
        op("scalar", ["SCR2"], ["ROWB"],
           lambda s, t0=t0: s.activation(out=n.ROWB[0:1, t0:t0 + 512], in_=n.SCR2[0:1, :], func=AF.Sqrt))

    for tt in range(NT):
        t0 = tt * 512
        op("tensor", ["ROWB", "ONESF"], ["P7"],
           lambda t, t0=t0: t.matmul(P7[:, :], n.ONESF[0:1, :], n.ROWB[0:1, t0:t0 + 512], start=True, stop=True))

        def ohat(v, t0=t0):
            for pr in range(8):
                i = v.scalar_tensor_tensor(QS[:, pr * T + t0:pr * T + t0 + 512],
                                           OT[:, pr * T + t0:pr * T + t0 + 512],
                                           n.NW[:, pr:pr + 1], P7[:, :], ALU.mult, ALU.mult)
            return i
        op("vector", ["OT", "NW", "P7", "QS"], ["QS"], ohat)

    op("gpsimd", ["aWo"], ["W1"],
       lambda g2: g2.dma_start(out=n.W1.rearrange("p (c f) -> p c f", c=8),
                               in_=n.aW["o"].rearrange("(c p) f -> c p f", c=8).transpose([1, 0, 2])),
       chan=k.chan("w1"))

    for fc in range(8):
        for tt in range(NT):
            t0 = tt * 512
            bn = ["P0", "P1", "P2", "P3"][(fc * NT + tt) % 4]
            bank = n.PS[bn]

            def womm(t, fc=fc, t0=t0, bank=bank):
                for pr in range(8):
                    i = t.matmul(bank[:, :],
                                 n.W1[:, pr * 1024 + fc * 128:pr * 1024 + fc * 128 + 128],
                                 QS[:, pr * T + t0:pr * T + t0 + 512],
                                 start=(pr == 0), stop=(pr == 7))
                return i
            op("tensor", ["W1", "QS"], [bn], womm)
            res_write(n, n.x_in, "x_in", n.xfp.ap(), "xfp", fc, t0, fc * NT + tt, bank, bn)


def fprep(n):
    k, op = n.k, n.k.op
    P0 = n.P0

    def scw(ft):
        row = n.ROWA if ft < 4 else n.ROWB
        nm = "ROWA" if ft < 4 else "ROWB"
        c0 = (ft % 4) * 512
        return row[0:1, c0:c0 + 512], nm

    op("gpsimd", [], ["QS"], lambda g: g.dma_start(out=n.FUTS, in_=n.fuT[:, :]), chan=k.chan("cn7"))
    op("gpsimd", [], ["OT"], lambda g: g.dma_start(out=n.FVB, in_=n.fv[:, :]), chan=k.chan("cn4"))
    for ft in range(8):
        op("tensor", ["FFB", "QS"], ["P0"],
           lambda t, ft=ft: t.matmul(P0[0:16, :], n.FFB[:, :], n.FUTS[:, ft * 512:(ft + 1) * 512],
                                     start=True, stop=True))
        op("vector", ["P0"], ["QS"],
           lambda v, ft=ft: v.tensor_copy(n.FT1[:, ft * 512:(ft + 1) * 512], P0[0:16, :]))

    strips = [(n.XT32A, "XT32A"), (n.XT32B, "XT32B"), (n.XO32A, "XO32A"), (n.XO32B, "XO32B")]

    for wtag, wsrc, nchunk, nf, dstd in (
            ("fc", n.fwfcT, 8, 4096, n.weff_fc),
            ("pj", n.fwprojT, 32, 1024, n.weff_pj)):
        nft = nf // 512
        recip = 1.0 / (128 * nchunk)
        for ft in range(nft):
            rw, rn = scw(ft)
            op("vector", [], [rn], lambda v, rw=rw: v.memset(rw, 0.0))
        si = 0
        for c in range(nchunk):
            for ft in range(nft):
                sl = slice(ft * 512, (ft + 1) * 512)
                S, sn = strips[si % 4]
                si += 1
                op("gpsimd", [], [sn],
                   lambda g, c=c, sl=sl, S=S: g.dma_start(out=S[:], in_=wsrc[c * 128:(c + 1) * 128, sl]),
                   chan=k.chan("wp%d" % (si % 2)))
                op("scalar", [sn], [sn],
                   lambda s, S=S: s.activation(out=S[:], in_=S[:], func=AF.Abs))
                rw, rn = scw(ft)
                op("tensor", [sn, "ONESF"], ["P0"],
                   lambda t, S=S: t.matmul(P0[0:1, :], n.ONESF[:, 0:1], S[:], start=True, stop=True))
                op("vector", ["P0", rn], [rn], lambda v, rw=rw: v.tensor_add(rw, rw, P0[0:1, :]))
        for ft in range(nft):
            rw, rn = scw(ft)
            op("vector", [rn], [rn], lambda v, rw=rw: v.tensor_scalar_mul(rw, rw, recip))
            op("vector", [rn], [rn], lambda v, rw=rw: v.tensor_scalar_max(rw, rw, 1e-5))
        for c in range(nchunk):
            for ft in range(nft):
                sl = slice(ft * 512, (ft + 1) * 512)
                S, sn = strips[si % 4]
                si += 1
                op("gpsimd", [], [sn],
                   lambda g, c=c, sl=sl, S=S: g.dma_start(out=S[:], in_=wsrc[c * 128:(c + 1) * 128, sl]),
                   chan=k.chan("wp%d" % (si % 2)))
                rw, rn = scw(ft)
                op("tensor", [rn, "ONESF"], ["P6"],
                   lambda t, rw=rw: t.matmul(n.P6[:, :], n.ONESF[0:1, :], rw, start=True, stop=True))
                op("scalar", [sn, "KD"], ["KD"],
                   lambda s, S=S: s.activation(out=n.KD[:, 512:1024], in_=S[:], func=AF.Sign))
                op("scalar", [sn], [sn],
                   lambda s, S=S: s.activation(out=S[:], in_=S[:], func=AF.Abs, scale=2.0))
                op("vector", [sn, "P6", "KD"], ["KD"],
                   lambda v, S=S: v.tensor_tensor(n.KD[:, 0:512], S[:], n.P6[:, :], ALU.is_gt))
                op("vector", ["P6", "KD"], ["KD"],
                   lambda v: v.tensor_mul(n.KD[:, 0:512], n.KD[:, 0:512], n.P6[:, :]))
                op("vector", ["KD"], ["KD"],
                   lambda v: v.tensor_mul(n.KD[:, 1024:1536], n.KD[:, 0:512], n.KD[:, 512:1024]))
                if wtag == "fc":
                    op("tensor", ["OT", "QS"], ["P7"],
                       lambda t, c=c, sl=sl: t.matmul(n.P7[:, :], n.FVB[:, c * 128:(c + 1) * 128],
                                                      n.FT1[:, sl], start=True, stop=True))
                    op("vector", ["P7", "GATEB", "KD"], ["KD"],
                       lambda v: v.scalar_tensor_tensor(n.KD[:, 1024:1536], n.P7[:, :], n.GATEB[:, 0:1],
                                                        n.KD[:, 1024:1536], ALU.mult, ALU.add))
                op("gpsimd", ["KD"], ["weff_" + wtag],
                   lambda g, c=c, sl=sl, dstd=dstd: g.dma_start(
                       out=dstd.ap()[c * 128:(c + 1) * 128, sl], in_=n.KD[:, 1024:1536]),
                   chan=k.chan("we"))


def ffn(n, src, srcname, dst, dstname, tag):
    k, op = n.k, n.k.op
    QS = n.QS
    ln_full(n, 1e-5)
    for tt in range(NT):
        t0 = tt * 512
        ln_apply(n, t0)
        for wc in range(4):
            wslot = n.W0 if wc % 2 == 0 else n.W1
            wname = "W0" if wc % 2 == 0 else "W1"
            op("gpsimd", ["weff_fc"], [wname],
               lambda g, wc=wc, wslot=wslot: g.dma_start(
                   out=wslot.rearrange("p (c f) -> p c f", c=8),
                   in_=n.weff_fc.ap()[:, wc * 1024:(wc + 1) * 1024]
                       .rearrange("(c p) f -> c p f", c=8).transpose([1, 0, 2])),
               chan=k.chan("wf" + str(wc % 2)))
            for fcl in range(8):
                fc = wc * 8 + fcl
                bn = ["P0", "P1", "P2", "P3"][fc % 4]
                bank = n.PS[bn]

                def umm(t, fcl=fcl, bank=bank, wslot=wslot):
                    for dc in range(8):
                        i = t.matmul(bank[:, :],
                                     wslot[:, dc * 1024 + fcl * 128:dc * 1024 + fcl * 128 + 128],
                                     n.XH[:, dc * 512:(dc + 1) * 512],
                                     start=(dc == 0), stop=(dc == 7))
                    return i
                op("tensor", [wname, "XH"], [bn], umm)
                op("scalar", [bn], ["QS"],
                   lambda s, fc=fc, bank=bank: s.activation(out=QS[:, fc * 512:(fc + 1) * 512],
                                                            in_=bank[:, :], func=AF.Relu))
                op("vector", ["QS"], ["QS"],
                   lambda v, fc=fc: v.tensor_mul(QS[:, fc * 512:(fc + 1) * 512],
                                                 QS[:, fc * 512:(fc + 1) * 512],
                                                 QS[:, fc * 512:(fc + 1) * 512]))
        for fc in range(8):
            wslot = n.W0 if fc % 2 == 0 else n.W1
            wname = "W0" if fc % 2 == 0 else "W1"
            op("gpsimd", ["weff_pj"], [wname],
               lambda g, fc=fc, wslot=wslot: g.dma_start(
                   out=wslot[:, 0:4096].rearrange("p (c f) -> p c f", c=32),
                   in_=n.weff_pj.ap()[:, fc * 128:(fc + 1) * 128]
                       .rearrange("(c p) f -> c p f", c=32).transpose([1, 0, 2])),
               chan=k.chan("wf" + str(fc % 2)))
            bn = ["P0", "P1", "P2", "P3"][fc % 4]
            bank = n.PS[bn]

            def dmm(t, bank=bank, wslot=wslot):
                for kc in range(32):
                    i = t.matmul(bank[:, :],
                                 wslot[:, kc * 128:kc * 128 + 128],
                                 QS[:, kc * 512:(kc + 1) * 512],
                                 start=(kc == 0), stop=(kc == 31))
                return i
            op("tensor", [wname, "QS"], [bn], dmm)
            res_write(n, src, srcname, dst, dstname, fc, t0, fc, bank, bn)


def attnb(n, dst, dstname, stop=None):
    k, op = n.k, n.k.op
    P0, P1, P2, P3, P4, P6, P7 = n.P0, n.P1, n.P2, n.P3, n.P4, n.P6, n.P7

    for nm, dstt in (("bqVb", n.QVB), ("bkVb", n.KVB)):
        op("gpsimd", [], ["AM"], lambda g, nm=nm, dstt=dstt: g.dma_start(
            out=dstt[:, :], in_=getattr(n, nm)[:, :]), chan=k.chan("cn9"))

    _sid = n.nc.enter_named_scope("ab_proj", False)[0]
    MP = lambda d: n.VTOK[:, d * 512:(d + 1) * 512]
    MD0 = n.VTOK[:, 4 * 512:5 * 512]
    MD1 = n.VTOK[:, 5 * 512:6 * 512]
    TB = n.VTOK[:, 6 * 512:7 * 512]
    for d in range(4):
        op("gpsimd", [], ["VTOK"], lambda g, d=d: g.memset(MP(d), 1.0))
        op("gpsimd", ["VTOK"], ["VTOK"],
           lambda g, d=d: g.affine_select(out=MP(d), in_=MP(d), compare_op=ALU.is_ge,
                                          fill=0.0, base=-128 * d, pattern=[[1, 512]],
                                          channel_multiplier=-1))
    op("vector", ["ROLEF", "TMP32"], ["TMP32"],
       lambda v: v.tensor_scalar(n.TMP32[:, 0:1], n.ROLEF[:, :], -1.0, 1.0, ALU.mult, ALU.add))
    for (mdst, pa, pb) in ((MD0, MP(0), MP(1)), (MD1, MP(2), MP(3))):
        op("vector", ["VTOK", "ROLEF"], ["VTOK"],
           lambda v, pb=pb: v.tensor_scalar(TB, pb, n.ROLEF[0:128, 0:1], None, ALU.mult))
        op("vector", ["VTOK", "TMP32"], ["VTOK"],
           lambda v, pa=pa, mdst=mdst: v.scalar_tensor_tensor(mdst, pa, n.TMP32[:, 0:1], TB,
                                                              ALU.mult, ALU.add))

    ln_full(n, 1e-5)
    for tt in range(NT):
        t0 = tt * 512
        ln_apply(n, t0)
        for c in range(8):
            op("vector", ["XH"], ["OT"],
               lambda v, c=c, t0=t0: v.tensor_copy(n.OT[:, c * T + t0:c * T + t0 + 512],
                                                   n.XH[:, c * 512:(c + 1) * 512]))
    ln_full(n, 1e-5, srct=n.OT, srcn="OT")
    for tt in range(NT):
        t0 = tt * 512
        ln_apply(n, t0, srct=n.OT, srcn="OT", gt=n.LN1G, gn="LN1G", bt=n.LN1B, bn="LN1B")
        for (wap, wn, agin, agn) in ((n.bqU, "bqUd", n.pq_in, "pq_in"),
                                     (n.bkU, "bkUd", n.pk_in, "pk_in"),
                                     (n.bvU, "bvUd", n.pv_in, "pv_in")):
            op("gpsimd", [wn], ["W0"],
               lambda g, wap=wap: g.dma_start(out=n.W0.rearrange("p (c f) -> p c f", c=8),
                                              in_=wap.rearrange("(c p) f -> c p f", c=8).transpose([1, 0, 2])),
               chan=k.chan("w0"))
            for fc in range(8):
                bn_ = ["P0", "P1", "P2", "P3"][fc % 4]
                bank = n.PS[bn_]

                def pmm(t, fc=fc, bank=bank):
                    for dc in range(8):
                        i = t.matmul(bank[:, :],
                                     n.W0[:, dc * 1024 + fc * 128:dc * 1024 + fc * 128 + 128],
                                     n.XH[:, dc * 512:(dc + 1) * 512],
                                     start=(dc == 0), stop=(dc == 7))
                    return i
                op("tensor", ["W0", "XH"], [bn_], pmm)
                stg = n.KD[:, (fc % 2) * 512:(fc % 2) * 512 + 512]
                op("vector", [bn_], ["KD"],
                   lambda v, bank=bank, stg=stg: v.tensor_copy(stg, bank[:, :]))
                op("vector", ["KD", "ROWSEL"], ["KD"],
                   lambda v, stg=stg: v.tensor_scalar(stg, stg, n.ROWSEL[:, 0:1], None, ALU.add))
                if agn == "pq_in":
                    op("gpsimd", ["KD"], [agn],
                       lambda g, fc=fc, t0=t0, stg=stg, agin=agin: g.dma_start(
                           out=agin.ap()[fc * 128:(fc + 1) * 128, t0:t0 + 512], in_=stg),
                       chan=k.chan("ag%d" % (fc % 2)))
                else:
                    for par in range(2):
                        op("gpsimd", ["KD"], [agn],
                           lambda g, fc=fc, tt=tt, par=par, stg=stg, agin=agin: g.dma_start(
                               out=agin.ap()[fc * 128:(fc + 1) * 128,
                                             par * 1024 + 2 * tt * 128:par * 1024 + (2 * tt + 2) * 128]
                                   .rearrange("p (q w) -> p q w", q=2, w=128),
                               in_=stg.rearrange("p (u par w) -> p par u w", u=2, par=2, w=128)[:, par]),
                           chan=k.chan("ag%d" % (fc % 2)))

    n.nc.leave_named_scope("ab_proj", _sid, False)
    _sid = n.nc.enter_named_scope("ab_coll", False)[0]
    sts = [k.state.get(nm) for nm in ("pq_in", "pk_in", "pv_in")]

    def coll_body(g2):
        for st in sts:
            if st and st["w"]:
                k._wait(g2, "gpsimd", *st["w"])
        for pin, pout in ((n.pk_in, n.pk_out), (n.pv_in, n.pv_out), (n.pq_in, n.pq_out)):
            i = g2.collective_compute("AllGather", ALU.bypass,
                                      replica_groups=[[0, 1], [2, 3], [4, 5], [6, 7]],
                                      ins=[pin.ap().opt()], outs=[pout.ap().opt()])
            k.cc.inc(i, 1)
    k.block.gpsimd(coll_body)
    for nm in ("pq_out", "pk_out", "pv_out"):
        k.state[nm] = {"w": (k.cc, k.cc.n), "r": {}}


    n.nc.leave_named_scope("ab_coll", _sid, False)
    _sid = n.nc.enter_named_scope("ab_cache", False)[0]
    KVB_B = n.KD[:, 0:512]
    KVB_A = n.KD[:, 512:1024]
    VVB_B = n.KD[:, 1024:1544]
    VVB_A = n.KD[:, 1544:2064]
    op("vector", ["AM", "ROLEF"], ["KD"],
       lambda v: v.tensor_scalar(KVB_B, n.KVB[:, :], n.ROLEF[:, 0:1], None, ALU.mult))
    op("vector", ["AM", "KD"], ["KD"],
       lambda v: v.tensor_sub(KVB_A, n.KVB[:, :], KVB_B))
    op("vector", ["VVB", "ROLEF"], ["KD"],
       lambda v: v.tensor_scalar(VVB_B, n.VVB[:, :], n.ROLEF[:, 0:1], None, ALU.mult))
    op("vector", ["VVB", "KD"], ["KD"],
       lambda v: v.tensor_sub(VVB_A, n.VVB[:, :], VVB_B))

    def par_dma(agout, agn, rank, par, wt, wn):
        op("gpsimd", [agn], [wn],
           lambda g: g.dma_start(
               out=wt.rearrange("p (c q) -> p c q", c=8),
               in_=agout.ap()[rank][:, par * 1024:(par + 1) * 1024]
                   .rearrange("(c p) q -> c p q", c=8).transpose([1, 0, 2])),
           chan=k.chan("kb0"))

    for rank in range(2):
        par_dma(n.pk_out, "pk_out", rank, 0, n.W0, "W0")
        par_dma(n.pk_out, "pk_out", rank, 1, n.W1, "W1")
        for pr in range(8):
            for h0 in (0, 512):
                def kmm(t, pr=pr, h0=h0):
                    cs = pr * 1024 + h0
                    t.matmul(P4[0:64, 0:512], KVB_A[0:49, pr * 64:(pr + 1) * 64],
                             n.W0[0:49, cs:cs + 512], start=True, stop=False)
                    t.matmul(P4[0:64, 0:512], KVB_B[0:49, pr * 64:(pr + 1) * 64],
                             n.W1[0:49, cs:cs + 512], start=False, stop=True)
                    t.matmul(P4[64:128, 0:512], KVB_A[64:113, pr * 64:(pr + 1) * 64],
                             n.W0[64:113, cs:cs + 512], start=True, stop=False,
                             tile_position=(64, 64))
                    return t.matmul(P4[64:128, 0:512], KVB_B[64:113, pr * 64:(pr + 1) * 64],
                                    n.W1[64:113, cs:cs + 512], start=False, stop=True,
                                    tile_position=(64, 64))
                op("tensor", ["KD", "W0", "W1"], ["P4"], kmm)
                s0 = rank * 8 + h0 // 128
                op("vector", ["P4"], ["QS"],
                   lambda v, pr=pr, s0=s0: v.tensor_copy(
                       n.QS[:, (pr * 16 + s0) * 128:(pr * 16 + s0) * 128 + 512],
                       P4[:, 0:512]))
    for rank in range(2):
        par_dma(n.pv_out, "pv_out", rank, 0, n.W0, "W0")
        par_dma(n.pv_out, "pv_out", rank, 1, n.W1, "W1")
        for s8 in range(8):
            s = rank * 8 + s8
            for h in range(16):
                pr, hb = h // 2, h % 2
                rlo = 64 * hb
                bk = P6 if hb == 0 else P7
                bkn = "P6" if hb == 0 else "P7"

                def vmm(t, s8=s8, pr=pr, rlo=rlo, bk=bk):
                    cs = pr * 1024 + s8 * 128
                    t.matmul(bk[:, 0:65], n.W0[rlo:rlo + 49, cs:cs + 128],
                             VVB_A[rlo:rlo + 49, pr * 65:(pr + 1) * 65],
                             start=True, stop=False)
                    return t.matmul(bk[:, 0:65], n.W1[rlo:rlo + 49, cs:cs + 128],
                                    VVB_B[rlo:rlo + 49, pr * 65:(pr + 1) * 65],
                                    start=False, stop=True)
                op("tensor", ["KD", "W0", "W1"], [bkn], vmm)
                op("vector", [bkn], ["XBF"],
                   lambda v, s=s, h=h, bk=bk: v.tensor_copy(
                       n.XBF[:, (s * 16 + h) * 65:(s * 16 + h) * 65 + 65], bk[:, 0:65]))

    n.nc.leave_named_scope("ab_cache", _sid, False)
    if stop == "cache":
        return
    _sid = n.nc.enter_named_scope("ab_attn", False)[0]
    for eng in ("scalar", "vector", "tensor", "gpsimd"):
        k.wait_all(eng, ["KD", "W0"])
    qt_order = [0, 4, 1, 5, 2, 6, 3, 7]
    for oi_, qt in enumerate(qt_order):
        rank, co = qt // 4, (qt % 4) * 512
        qtn = "QTa" if oi_ % 2 == 0 else "QTb"
        qto = 0 if oi_ % 2 == 0 else 4096
        op("gpsimd", ["pq_out"], ["XH"],
           lambda g, rank=rank, co=co: g.dma_start(
               out=n.XH[:, 0:4096].rearrange("p (c w) -> p c w", c=8),
               in_=n.pq_out.ap()[rank][:, co:co + 512]
                   .rearrange("(c p) w -> c p w", c=8).transpose([1, 0, 2])),
           chan=k.chan("kb1"))
        for pr in range(8):
            def qmm(t, pr=pr):
                t.matmul(P0[0:64, 0:512], n.QVB[0:49, pr * 64:(pr + 1) * 64],
                         n.XH[0:49, pr * 512:(pr + 1) * 512], start=True, stop=True)
                return t.matmul(P0[64:128, 0:512], n.QVB[64:113, pr * 64:(pr + 1) * 64],
                                n.XH[64:113, pr * 512:(pr + 1) * 512], start=True, stop=True,
                                tile_position=(64, 64))
            op("tensor", ["AM", "XH"], ["P0"], qmm)
            op("vector", ["P0"], [qtn],
               lambda v, pr=pr, qto=qto: v.tensor_copy(
                   n.W0[:, qto + pr * 512:qto + (pr + 1) * 512], P0[:, 0:512]))

        nslot = 2 * qt + 2
        j, half = qt % 4, qt // 4

        def emit_qk(pr, sj):
            sa, san = (P4, "P4") if sj % 2 == 0 else (P7, "P7")
            sb, sbn = (P6, "P6") if sj % 2 == 0 else (P1, "P1")

            def qk(t, pr=pr, sj=sj, sa=sa, sb=sb, qto=qto):
                kl = n.QS[:, (pr * 16 + sj) * 128:(pr * 16 + sj) * 128 + 128]
                qtl = n.W0[:, qto + pr * 512:qto + (pr + 1) * 512]
                t.matmul(sa[:, :], kl[0:64, :], qtl[0:64, :], start=True, stop=True)
                return t.matmul(sb[:, :], kl[64:128, :], qtl[64:128, :], start=True, stop=True,
                                tile_position=(64, 0))
            op("tensor", ["QS", qtn], [san, sbn], qk)

        emit_qk(0, 0)
        for pr in range(8):
            for sj in range(nslot):
                if sj + 1 < nslot:
                    emit_qk(pr, sj + 1)
                elif pr + 1 < 8:
                    emit_qk(pr + 1, 0)
                sa, san = (P4, "P4") if sj % 2 == 0 else (P7, "P7")
                sb, sbn = (P6, "P6") if sj % 2 == 0 else (P1, "P1")
                en = "KDe%d" % (sj % 4)
                ea = n.KD[:, (sj % 4) * 1024:(sj % 4) * 1024 + 512]
                eb = n.KD[:, (sj % 4) * 1024 + 512:(sj % 4) * 1024 + 1024]
                op("scalar", [san], [en],
                   lambda s, sa=sa, ea=ea: s.activation(out=ea, in_=sa[:, :], func=AF.Exp, scale=0.125))
                op("scalar", [sbn], [en],
                   lambda s, sb=sb, eb=eb: s.activation(out=eb, in_=sb[:, :], func=AF.Exp, scale=0.125))
                if sj >= nslot - 2:
                    md = MD0 if sj == nslot - 2 else MD1
                    op("vector", [en, "VTOK"], [en],
                       lambda v, ea=ea, md=md: v.tensor_mul(ea, ea, md))
                    op("vector", [en, "VTOK"], [en],
                       lambda v, eb=eb, md=md: v.tensor_mul(eb, eb, md))

                def av(t, pr=pr, sj=sj, ea=ea, eb=eb):
                    va = n.XBF[:, (sj * 16 + 2 * pr) * 65:(sj * 16 + 2 * pr) * 65 + 65]
                    vb = n.XBF[:, (sj * 16 + 2 * pr + 1) * 65:(sj * 16 + 2 * pr + 1) * 65 + 65]
                    t.matmul(P2[0:65, :], va, ea, start=(sj == 0), stop=(sj == nslot - 1))
                    return t.matmul(P3[0:65, :], vb, eb, start=(sj == 0), stop=(sj == nslot - 1))
                op("tensor", ["XBF", en], ["P2", "P3"], av)

            for (cbn, cb, hh) in (("P2", P2, 2 * pr), ("P3", P3, 2 * pr + 1)):
                tn = "TMPc%d" % (hh % 2)
                tslc = n.TMP32[0:65, :] if hh % 2 == 0 else n.XT32A[0:65, :]
                op("vector", [cbn], [tn],
                   lambda v, cb=cb, tslc=tslc: v.tensor_copy(tslc, cb[0:65, :]))
                op("gpsimd", [tn], ["rs_in%d" % j],
                   lambda g, j=j, half=half, hh=hh, tslc=tslc: g.dma_start(
                       out=n.rs_in[j].ap()[half, hh], in_=tslc),
                   chan=k.chan("cs%d" % (pr % 2)))
        if oi_ % 2 == 1:
            st = k.state.get("rs_in%d" % j)

            def rs_body(g2, j=j, st=st):
                if st and st["w"]:
                    k._wait(g2, "gpsimd", *st["w"])
                i = g2.collective_compute("ReduceScatter", ALU.add,
                                          replica_groups=[[0, 1], [2, 3], [4, 5], [6, 7]],
                                          ins=[n.rs_in[j].ap().opt()], outs=[n.rs_out[j].ap().opt()])
                k.cc.inc(i, 1)
            k.block.gpsimd(rs_body)
            k.state["rs_out%d" % j] = {"w": (k.cc, k.cc.n), "r": {}}

    n.nc.leave_named_scope("ab_attn", _sid, False)
    _sid = n.nc.enter_named_scope("ab_outp", False)[0]
    for eng in ("scalar", "vector", "tensor", "gpsimd", "sync"):
        k.wait_all(eng, ["KDe0", "KDe1", "KDe2", "KDe3", "QTa", "QTb", "TMPc0", "TMPc1"])
    op("gpsimd", ["boutUd"], ["W0"],
       lambda g: g.dma_start(out=n.W0[:, 0:6144].rearrange("p (c f) -> p c f", c=8),
                             in_=n.boutU.rearrange("(c p) f -> c p f", c=8).transpose([1, 0, 2])),
       chan=k.chan("w0"))
    op("gpsimd", ["boutVd"], ["W1"],
       lambda g: g.dma_start(out=n.W1[:, 0:6144].rearrange("p (c f) -> p c f", c=6),
                             in_=n.boutV.rearrange("(c p) f -> c p f", c=6).transpose([1, 0, 2])),
       chan=k.chan("w1"))
    for j in range(4):
        t0 = j * 512
        for pr in range(8):
            op("gpsimd", ["rs_out%d" % j], ["XH"],
               lambda g, j=j, pr=pr: g.dma_start(
                   out=n.XH[0:64, pr * 512:(pr + 1) * 512], in_=n.rs_out[j].ap()[2 * pr, 0:64]),
               chan=k.chan("cl0"))
            op("gpsimd", ["rs_out%d" % j], ["XH"],
               lambda g, j=j, pr=pr: g.dma_start(
                   out=n.XH[64:128, pr * 512:(pr + 1) * 512], in_=n.rs_out[j].ap()[2 * pr + 1, 0:64]),
               chan=k.chan("cl1"))
        op("gpsimd", ["rs_out%d" % j], ["AM"],
           lambda g, j=j: g.dma_start(out=n.SUMS[:, :], in_=n.rs_out[j].ap()[:, 64, :]),
           chan=k.chan("cl2"))
        op("vector", ["AM"], ["TMP32"], lambda v: v.tensor_copy(n.SUMSF[:, :], n.SUMS[:, :]))
        op("vector", ["TMP32"], ["TMP32"], lambda v: v.reciprocal(n.SUMSF[:, :], n.SUMSF[:, :]))
        op("vector", ["TMP32"], ["AM"], lambda v: v.tensor_copy(n.SUMS[:, :], n.SUMSF[:, :]))
        for pr in range(8):
            op("tensor", ["SELB", "AM"], ["P4"],
               lambda t, pr=pr: t.matmul(P4[:, :], n.SELB[:, pr * 128:(pr + 1) * 128],
                                         n.SUMS[:, :], start=True, stop=True))
            op("vector", ["P4", "XH"], ["XH"],
               lambda v, pr=pr: v.tensor_mul(n.XH[:, pr * 512:(pr + 1) * 512],
                                             n.XH[:, pr * 512:(pr + 1) * 512], P4[:, :]))
        for mc in range(6):
            def t1mm(t, mc=mc):
                for dc in range(8):
                    i = t.matmul(P6[:, :], n.W0[:, dc * 768 + mc * 128:dc * 768 + mc * 128 + 128],
                                 n.XH[:, dc * 512:(dc + 1) * 512], start=(dc == 0), stop=(dc == 7))
                return i
            op("tensor", ["W0", "XH"], ["P6"], t1mm)
            op("vector", ["P6"], ["KD"],
               lambda v, mc=mc: v.tensor_copy(n.KD[:, mc * 512:(mc + 1) * 512], P6[:, :]))
        for fc in range(8):
            def aomm(t, fc=fc):
                for kc in range(6):
                    i = t.matmul(P7[:, :], n.W1[:, kc * 1024 + fc * 128:kc * 1024 + fc * 128 + 128],
                                 n.KD[:, kc * 512:(kc + 1) * 512], start=(kc == 0), stop=(kc == 5))
                return i
            op("tensor", ["W1", "KD"], ["P7"], aomm)
            op("vector", ["P7", "OUTB", "OT"], ["OT"],
               lambda v, fc=fc, t0=t0: v.scalar_tensor_tensor(
                   n.OT[:, fc * T + t0:fc * T + t0 + 512], P7[:, :], n.OUTB[:, fc:fc + 1],
                   n.OT[:, fc * T + t0:fc * T + t0 + 512], ALU.add, ALU.add))

    n.nc.leave_named_scope("ab_outp", _sid, False)
    _sid = n.nc.enter_named_scope("ab_mlp", False)[0]
    for eng in ("scalar", "gpsimd"):
        k.wait_all(eng, ["QS", "VTOK"])
    ln_full(n, 1e-5, srct=n.OT, srcn="OT")
    op("gpsimd", ["bfc1Ud"], ["W0"],
       lambda g: g.dma_start(out=n.W0[:, 0:4096].rearrange("p (c f) -> p c f", c=8),
                             in_=n.bfc1U.rearrange("(c p) f -> c p f", c=8).transpose([1, 0, 2])),
       chan=k.chan("w0"))
    op("gpsimd", ["bfc2Vd"], ["W0"],
       lambda g: g.dma_start(out=n.W0[:, 4096:8192].rearrange("p (c f) -> p c f", c=4),
                             in_=n.bfc2V.rearrange("(c p) f -> c p f", c=4).transpose([1, 0, 2])),
       chan=k.chan("w0b"))
    op("gpsimd", ["bfc1Vd"], ["W1"],
       lambda g: g.dma_start(
           out=n.W1[:, 0:8192].rearrange("p (c f) -> p c f", c=4),
           in_=n.bfc1V[:, 0:2048].rearrange("(c p) f -> c p f", c=4).transpose([1, 0, 2])),
       chan=k.chan("w1"))
    op("gpsimd", ["bfc1Vd"], ["WM"],
       lambda g: g.dma_start(
           out=n.WM[:, 0:8192].rearrange("p (c f) -> p c f", c=4),
           in_=n.bfc1V[:, 2048:4096].rearrange("(c p) f -> c p f", c=4).transpose([1, 0, 2])),
       chan=k.chan("wm"))
    for j in range(4):
        t0 = j * 512
        ln_apply(n, t0, srct=n.OT, srcn="OT", gt=n.LN2G, gn="LN2G", bt=n.LN2B, bn="LN2B")
        for mc in range(4):
            def m1(t, mc=mc):
                for dc in range(8):
                    i = t.matmul(P0[:, :], n.W0[:, dc * 512 + mc * 128:dc * 512 + mc * 128 + 128],
                                 n.XH[:, dc * 512:(dc + 1) * 512], start=(dc == 0), stop=(dc == 7))
                return i
            op("tensor", ["W0", "XH"], ["P0"], m1)
            op("vector", ["P0"], ["KD"],
               lambda v, mc=mc: v.tensor_copy(n.KD[:, mc * 512:(mc + 1) * 512], P0[:, :]))
        ACC = [(P4, "P4"), (P6, "P6"), (P7, "P7"), (P2, "P2")]

        def emit_m2(fc):
            WV, WVN = (n.W1, "W1") if fc < 16 else (n.WM, "WM")
            fcl = fc % 16
            bk2, bk2n = (P1, "P1") if fc % 2 == 0 else (P3, "P3")
            fun = "FU%d" % (fc % 8)
            fsl = n.VTOK[:, (fc % 8) * 512:(fc % 8) * 512 + 512]
            op("gpsimd", ["bfc2Ud"], [fun],
               lambda g, fc=fc, fsl=fsl: g.dma_start(
                   out=fsl, in_=n.bfc2U[fc * 128:(fc + 1) * 128, :]),
               chan=k.chan("fu"))

            def m2(t, fcl=fcl, WV=WV, bk2=bk2):
                for kc in range(4):
                    i = t.matmul(bk2[:, :], WV[:, kc * 2048 + fcl * 128:kc * 2048 + fcl * 128 + 128],
                                 n.KD[:, kc * 512:(kc + 1) * 512], start=(kc == 0), stop=(kc == 3))
                return i
            op("tensor", [WVN, "KD"], [bk2n], m2)

        emit_m2(0)
        for fc in range(32):
            if fc + 1 < 32:
                emit_m2(fc + 1)
            bk2, bk2n = (P1, "P1") if fc % 2 == 0 else (P3, "P3")
            fun = "FU%d" % (fc % 8)
            fsl = n.VTOK[:, (fc % 8) * 512:(fc % 8) * 512 + 512]
            qgn = "QSg%d" % (fc % 8)
            qsl = n.QS[:, (fc % 8) * 512:(fc % 8) * 512 + 512]
            op("scalar", [bk2n, "FC1B"], [qgn],
               lambda s, fc=fc, qsl=qsl, bk2=bk2: s.activation(out=qsl, in_=bk2[:, :],
                                                               func=AF.Gelu_apprx_tanh,
                                                               bias=n.FC1B[:, fc:fc + 1]))
            for mc in range(4):
                bk, bkn = ACC[mc]

                def m3(t, fc=fc, mc=mc, bk=bk, fsl=fsl, qsl=qsl):
                    return t.matmul(bk[:, :], fsl[:, mc * 128:(mc + 1) * 128], qsl,
                                    start=(fc == 0), stop=(fc == 31))
                op("tensor", [fun, qgn], [bkn], m3)
        for mc in range(4):
            bk, bkn = ACC[mc]
            op("vector", [bkn], ["KD"],
               lambda v, mc=mc, bk=bk: v.tensor_copy(n.KD[:, mc * 512:(mc + 1) * 512], bk[:, :]))
        for fc in range(8):
            def m4(t, fc=fc):
                for kc in range(4):
                    i = t.matmul(P3[:, :], n.W0[:, 4096 + kc * 1024 + fc * 128:4096 + kc * 1024 + fc * 128 + 128],
                                 n.KD[:, kc * 512:(kc + 1) * 512], start=(kc == 0), stop=(kc == 3))
                return i
            op("tensor", ["W0", "KD"], ["P3"], m4)
            r = fc % 2
            XT = n.XT32A if r == 0 else n.XT32B
            XO = n.XO32A if r == 0 else n.XO32B
            xtn = "XT32%s" % ("A" if r == 0 else "B")
            xon = "XO32%s" % ("A" if r == 0 else "B")
            op("sync", ["xfp"], [xtn],
               lambda s, fc=fc, t0=t0, XT=XT: s.dma_start(
                   out=XT[:], in_=n.xfp.ap()[fc * 128:(fc + 1) * 128, t0:t0 + 512]),
               chan=k.chan("xt%d" % r))
            op("vector", ["P3", "FC2B", "OT", xtn], [xon],
               lambda v, fc=fc, t0=t0, XO=XO: v.scalar_tensor_tensor(
                   XO[:], n.P3[:, :], n.FC2B[:, fc:fc + 1],
                   n.OT[:, fc * T + t0:fc * T + t0 + 512], ALU.add, ALU.add))
            op("vector", [xon, xtn], [xon],
               lambda v, XO=XO, XT=XT: v.tensor_add(XO[:], XO[:], XT[:]))
            op("vector", [xon], ["XBF"],
               lambda v, fc=fc, t0=t0, XO=XO: v.tensor_copy(
                   n.XBF[:, fc * T + t0:fc * T + t0 + 512], XO[:]))
            op("sync", [xon], [dstname],
               lambda s, fc=fc, t0=t0, XO=XO: s.dma_start(
                   out=dst[fc * 128:(fc + 1) * 128, t0:t0 + 512], in_=XO[:]),
               chan=k.chan("xo%d" % r))

    slotnames = (["QSg%d" % i for i in range(8)] + ["FU%d" % i for i in range(8)] +
                 ["WM", "KDe0", "KDe1", "KDe2", "KDe3", "QTa", "QTb", "TMPc0", "TMPc1"])
    for eng in ("scalar", "vector", "tensor", "gpsimd", "sync"):
        k.wait_all(eng, slotnames)
    n.nc.leave_named_scope("ab_mlp", _sid, False)


def emit_debug(n, dbg_names):
    k, op = n.k, n.k.op
    dbg_map = {
        "TPOS": (n.TPOS, "TPOS"), "CUM": (n.CUM, "CUM"), "EQ": (n.EQ, "EQ"), "EK": (n.EK, "EK"),
        "QS": (n.QS, "QS"), "KD": (n.KD, "KD"), "OT": (n.OT, "OT"),
        "SST": (n.SST, "SST"), "CLALL": (n.CLALL, "CLALL"), "DD": (n.DD, "DD"),
        "XBF": (n.XBF, "XBF"), "XH": (n.XH, "XH"), "AM": (n.AM, "AM"), "PST": (n.PST, "PST"),
        "ROWA": (n.ROWA, "ROWA"), "ROWB": (n.ROWB, "ROWB"),
        "W1": (n.W1, "W1"), "W0": (n.W0, "W0"), "VTOK": (n.VTOK, "VTOK"),
    }
    for nm in dbg_names:
        buf, res = dbg_map[nm]
        o = n.nc.dram_tensor("dbg_" + nm, list(buf.shape), buf.dtype, kind="ExternalOutput").ap()
        op("gpsimd", [res], ["dbg_" + nm],
           lambda g2, o=o, buf=buf: g2.dma_start(out=o, in_=buf[:]), chan=k.chan("dbg"))


def fin(n):
    k = n.k

    def _fin(g2):
        for cname in ("xo0", "xo1", "dbg", "we"):
            if cname in k.chans:
                c = k.chans[cname]
                g2.wait_ge(c.h, c.n)
    k.block.gpsimd(_fin)


def host_inputs(inp, r):
    p, role = r // 2, r % 2
    d = {}
    x = np.asarray(inp["L0_attn_a_x"])
    d["x"] = np.ascontiguousarray(x[p].T[:, role * T:(role + 1) * T])
    d["rolef"] = np.full((128, 1), float(role), dtype=np.float32)
    for nm in "qkvo":
        d["aW" + nm] = np.ascontiguousarray(np.asarray(inp["L0_attn_a_W" + nm]).T)
    d["aWf"] = np.ascontiguousarray(np.asarray(inp["L0_attn_a_Wf"]).T)
    d["adelta"] = np.ascontiguousarray(np.asarray(inp["L0_attn_a_delta"]).reshape(16, 1))
    d["anw"] = np.ascontiguousarray(np.asarray(inp["L0_attn_a_norm_weight"]).reshape(8, 128).T)
    d["fwfcT"] = np.ascontiguousarray(np.asarray(inp["L0_ffn_w_fc"]).T)
    d["fwprojT"] = np.ascontiguousarray(np.asarray(inp["L0_ffn_w_proj"]).T)
    d["fuT"] = np.ascontiguousarray(np.asarray(inp["L0_ffn_u"]).T)
    d["fv"] = np.ascontiguousarray(np.asarray(inp["L0_ffn_v"]))
    d["ffb"] = np.ascontiguousarray(np.asarray(inp["L0_ffn_fast_b"]))
    d["fgate"] = np.asarray(inp["L0_ffn_gate"]).reshape(1, 1).astype(np.float32)

    d["rolei"] = np.full((1, 1), role, dtype=np.int32)
    rs = np.zeros((128, 1), np.float32)
    rs[48, 0] = 1.0
    rs[112, 0] = 1.0
    d["rowsel"] = rs
    col8 = lambda a: np.ascontiguousarray(np.asarray(a).reshape(8, 128).T)
    d["bln1g"] = col8(inp["L0_attn_b_ln1_g"])
    d["bln1b"] = col8(inp["L0_attn_b_ln1_b"])
    d["bln2g"] = col8(inp["L0_attn_b_ln2_g"])
    d["bln2b"] = col8(inp["L0_attn_b_ln2_b"])

    def packU(U):
        U = np.asarray(U)
        out = np.zeros((1024, 1024), np.float32)
        for fc in range(8):
            out[:, fc * 128 + 0:fc * 128 + 48] = U[:, 2 * fc, :]
            out[:, fc * 128 + 64:fc * 128 + 112] = U[:, 2 * fc + 1, :]
        return out

    def packVb(V, b, sums_col=False):
        V = np.asarray(V)
        b = np.asarray(b)
        w = 65 if sums_col else 64
        out = np.zeros((128, 8 * w), np.float32)
        for pr in range(8):
            out[0:48, pr * w:pr * w + 64] = V[2 * pr]
            out[48, pr * w:pr * w + 64] = b[2 * pr]
            out[64:112, pr * w:pr * w + 64] = V[2 * pr + 1]
            out[112, pr * w:pr * w + 64] = b[2 * pr + 1]
            if sums_col:
                out[48, pr * w + 64] = 1.0
                out[112, pr * w + 64] = 1.0
        return out

    d["bqU"] = packU(inp["L0_attn_b_q_U"])
    d["bkU"] = packU(inp["L0_attn_b_k_U"])
    d["bvU"] = packU(inp["L0_attn_b_v_U"])
    d["bqVb"] = packVb(inp["L0_attn_b_q_V"], inp["L0_attn_b_q_b"])
    d["bkVb"] = packVb(inp["L0_attn_b_k_V"], inp["L0_attn_b_k_b"])
    d["bvVb"] = packVb(inp["L0_attn_b_v_V"], inp["L0_attn_b_v_b"], sums_col=True)
    d["boutU"] = np.ascontiguousarray(np.asarray(inp["L0_attn_b_out_U"], np.float32))
    d["boutV"] = np.ascontiguousarray(np.asarray(inp["L0_attn_b_out_V"], np.float32))
    d["boutb"] = col8(inp["L0_attn_b_out_b"])
    d["bfc1U"] = np.ascontiguousarray(np.asarray(inp["L0_attn_b_fc1_U"], np.float32))
    d["bfc1V"] = np.ascontiguousarray(np.asarray(inp["L0_attn_b_fc1_V"], np.float32))
    d["bfc1b"] = np.ascontiguousarray(np.asarray(inp["L0_attn_b_fc1_b"]).reshape(32, 128).T)
    d["bfc2U"] = np.ascontiguousarray(np.asarray(inp["L0_attn_b_fc2_U"], np.float32))
    d["bfc2V"] = np.ascontiguousarray(np.asarray(inp["L0_attn_b_fc2_V"], np.float32))
    d["bfc2b"] = col8(inp["L0_attn_b_fc2_b"])
    return d




_CACHE = {}


def kernel(**inputs):
    if "nc" not in _CACHE:
        _CACHE["nc"] = build((), ("gla", "fprep", "ffn1", "attnb", "ffn2"))
    nc = _CACHE["nc"]
    ins = [host_inputs(inputs, r) for r in range(8)]
    res = run_bass_kernel_spmd(nc, ins, core_ids=list(range(8)))
    out = np.empty((4, 4096, 1024), np.float32)
    for r in range(8):
        p, role = r // 2, r % 2
        out[p, role * T:(role + 1) * T, :] = np.asarray(res.results[r]["xout"]).T
    return out


# version 03 (final)
# speedup vs baseline: 1.1874x; 1.1874x over previous
import numpy as np
import ml_dtypes
from contextlib import ExitStack
import concourse.bass as bass
import concourse.mybir as mybir
from concourse.bass_utils import run_bass_kernel_spmd

F32 = mybir.dt.float32
BF16 = mybir.dt.bfloat16
I32 = mybir.dt.int32
AF = mybir.ActivationFunctionType
ALU = mybir.AluOpType


class Sem:
    def __init__(self, es, nc, name):
        self.h = es.enter_context(nc.semaphore(name))
        self.n = 0
        self.name = name

    def inc(self, instr, k=1):
        instr.then_inc(self.h, k)
        self.n += k
        return self.n


class K:
    ENGS = ("tensor", "vector", "scalar", "gpsimd", "sync")

    def __init__(self, nc, es):
        self.nc = nc
        self.es = es
        self.sem = {e: Sem(es, nc, "s_" + e) for e in ("tensor", "vector", "scalar", "gpsimd")}
        self.cc = Sem(es, nc, "s_cc")
        self.chans = {}
        self.state = {}
        self.observed = {e: {} for e in self.ENGS}
        self.block = None

    def chan(self, name):
        if name not in self.chans:
            self.chans[name] = Sem(self.es, self.nc, "ch_" + name)
        return self.chans[name]

    def _wait(self, engobj, eng, sem, val):
        ob = self.observed[eng]
        if ob.get(sem.name, -1) >= val:
            return
        engobj.wait_ge(sem.h, val)
        ob[sem.name] = val

    def op(self, eng, reads, writes, emit, chan=None, inc=1):
        sem = chan if chan is not None else self.sem[eng]
        k = self

        def body(engobj):
            for t in reads:
                st = k.state.get(t)
                if st and st["w"]:
                    k._wait(engobj, eng, *st["w"])
            for t in writes:
                st = k.state.get(t)
                if st:
                    if st["w"]:
                        k._wait(engobj, eng, *st["w"])
                    for s_, v in list(st["r"].items()):
                        k._wait(engobj, eng, s_, v)
            if chan is not None:
                k._wait(engobj, eng, chan, chan.n)
            last = emit(engobj)
            sem.inc(last, 16 if chan is not None else inc)

        getattr(self.block, eng)(body)
        mark = (sem, sem.n)
        for t in writes:
            self.state[t] = {"w": mark, "r": {}}
        for t in reads:
            st = self.state.setdefault(t, {"w": None, "r": {}})
            st["r"][mark[0]] = mark[1]
        return mark

    def wait_all(self, eng, resources):
        def body(engobj):
            for t in resources:
                st = self.state.get(t)
                if not st:
                    continue
                if st["w"]:
                    self._wait(engobj, eng, *st["w"])
                for s_, v in list(st["r"].items()):
                    self._wait(engobj, eng, s_, v)
        getattr(self.block, eng)(body)


T = 2048
NT = 4
LCH = 4
_ES = []


class NS:
    pass


def build(dbg_names=(), phases=("gla", "fprep", "ffn1")):
    nc = bass.Bass("TRN2", target_bir_lowering=False, debug=False)
    es = ExitStack()
    _ES.append(es)
    k = K(nc, es)
    n = NS()
    n.nc, n.k = nc, k

    din = lambda nm, sh, dt=F32: nc.dram_tensor(nm, sh, dt, kind="ExternalInput").ap()
    dout = lambda nm, sh, dt=F32: nc.dram_tensor(nm, sh, dt, kind="ExternalOutput").ap()
    sb = lambda nm, sh, dt=BF16: es.enter_context(nc.sbuf_tensor(nm, sh, dt))

    n.x_in = din("x", [1024, T])
    n.rolef_in = din("rolef", [128, 1])
    n.aW = {nm: din("aW" + nm, [1024, 1024]) for nm in "qkvo"}
    n.aWf = din("aWf", [1024, 16])
    n.adelta = din("adelta", [16, 1])
    n.anw = din("anw", [128, 8])
    n.fwfcT = din("fwfcT", [1024, 4096])
    n.fwprojT = din("fwprojT", [4096, 1024])
    n.fuT = din("fuT", [16, 4096])
    n.fv = din("fv", [16, 1024])
    n.ffb = din("ffb", [16, 16])
    n.fgate = din("fgate", [1, 1])
    n.rolei_in = din("rolei", [1, 1], I32)
    n.rowsel_in = din("rowsel", [128, 1])
    n.bln1g = din("bln1g", [128, 8])
    n.bln1b = din("bln1b", [128, 8])
    n.bln2g = din("bln2g", [128, 8])
    n.bln2b = din("bln2b", [128, 8])
    n.bqU = din("bqU", [1024, 1024])
    n.bkU = din("bkU", [1024, 1024])
    n.bvU = din("bvU", [1024, 1024])
    n.bqVb = din("bqVb", [128, 512])
    n.bkVb = din("bkVb", [128, 512])
    n.bvVb = din("bvVb", [128, 520])
    n.boutU = din("boutU", [1024, 768])
    n.boutV = din("boutV", [768, 1024])
    n.boutb = din("boutb", [128, 8])
    n.bfc1U = din("bfc1U", [1024, 512])
    n.bfc1V = din("bfc1V", [512, 4096])
    n.bfc1b = din("bfc1b", [128, 32])
    n.bfc2U = din("bfc2U", [4096, 512])
    n.bfc2V = din("bfc2V", [512, 1024])
    n.bfc2b = din("bfc2b", [128, 8])
    n.x_out = dout("xout", [1024, T])

    n.xfp = nc.dram_tensor("xfp", [1024, T], F32)
    n.aWb = {nm: nc.dram_tensor("aWb" + nm, [1024, 1024], BF16) for nm in "qkvo"}
    n.fc2Ub = nc.dram_tensor("fc2Ub", [4096, 512], BF16)
    n.weff_fc = nc.dram_tensor("weff_fc", [1024, 4096], BF16)
    n.weff_pj = nc.dram_tensor("weff_pj", [4096, 1024], BF16)
    n.gla_in = nc.dram_tensor("gla_in", [1040, 64], F32)
    n.gla_out = nc.dram_tensor("gla_out", [2, 1040, 64], F32)
    n.pq_in = nc.dram_tensor("pq_in", [1024, T], BF16)
    n.pk_in = nc.dram_tensor("pk_in", [1024, T], BF16)
    n.pv_in = nc.dram_tensor("pv_in", [1024, T], BF16)
    n.pq_out = nc.dram_tensor("pq_out", [2, 1024, T], BF16)
    n.pk_out = nc.dram_tensor("pk_out", [2, 1024, T], BF16)
    n.pv_out = nc.dram_tensor("pv_out", [2, 1024, T], BF16)
    n.rs_in = [nc.dram_tensor("rs_in%d" % j, [2, 16, 65, 512], BF16) for j in range(4)]
    n.rs_out = [nc.dram_tensor("rs_out%d" % j, [16, 65, 512], BF16) for j in range(4)]

    n.XBF = sb("XBF", [128, 8 * T + 256])
    n.XH = sb("XH", [128, 8 * 512])
    n.QS = sb("QS", [128, 8 * T])
    n.OT = sb("OT", [128, 8 * T])
    n.KD = sb("KD", [128, 8 * 512])
    n.VTOK = sb("VTOK", [128, 4 * 1024])
    n.KDTOK = sb("KDTOK", [128, 512])
    n.AM = sb("AM", [128, 2 * 512])
    n.W0 = sb("W0", [128, 8 * 1024])
    n.W1 = sb("W1", [128, 8 * 1024])
    n.WF = sb("WF", [128, 8 * 16])
    n.SST = sb("SST", [128, 8 * 64], F32)
    n.SPREB = sb("SPREB", [128, 4 * 64])
    n.USB = sb("USB", [128, 4 * 64], F32)
    n.TPOS = sb("TPOS", [16, 512], F32)
    n.CUM = sb("CUM", [16, 512], F32)
    n.EQ = sb("EQ", [16, 512])
    n.EK = sb("EK", [16, 512])
    n.CLALL = sb("CLALL", [16, 16], F32)
    n.NEGCL = sb("NEGCL", [16, 16], F32)
    n.DD = sb("DD", [16, 16], F32)
    n.DDB = sb("DDB", [128, 8 * 4], F32)
    n.INCL = sb("INCL", [16, 16], F32)
    n.PREFE = sb("PREFE", [16, 16], F32)
    n.PREFB = sb("PREFB", [128, 8 * 16], F32)
    n.ONESB = sb("ONESB", [128, 1])
    n.ONES16 = sb("ONES16", [16, 128], F32)
    n.ONESF = sb("ONESF", [128, 128], F32)
    n.SELB = sb("SELB", [16, 8 * 128])
    n.SELF_ = sb("SELF", [16, 8 * 128], F32)
    n.TRIL = sb("TRIL", [128, 128])
    n.IDENT = sb("IDENT", [128, 128])
    n.NDELTA = sb("NDELTA", [16, 1], F32)
    n.NW = sb("NW", [128, 8], F32)
    n.ROLEF = sb("ROLEF", [128, 1], F32)
    n.ROWA = sb("ROWA", [1, T], F32)
    n.ROWB = sb("ROWB", [1, T], F32)
    n.SCR1 = sb("SCR1", [1, 512], F32)
    n.SCR2 = sb("SCR2", [1, 512], F32)
    n.TMP32 = sb("TMP32", [128, 512], F32)
    n.CLS = sb("CLS", [16, 1], F32)
    n.CP16 = sb("CP16", [16, 64], F32)
    n.PST = sb("PST", [128, 8 * 64], F32)
    n.PCB = sb("PCB", [128, 2 * 64])
    n.XT32A = sb("XT32A", [128, 512], F32)
    n.XT32B = sb("XT32B", [128, 512], F32)
    n.XO32A = sb("XO32A", [128, 512], F32)
    n.XO32B = sb("XO32B", [128, 512], F32)
    n.GATEB = sb("GATEB", [128, 1], F32)
    n.GATES = sb("GATES", [1, 1], F32)
    n.ROLEI = sb("ROLEI", [1, 1], I32)
    n.ROWSEL = sb("ROWSEL", [128, 1], F32)
    n.LN1G = sb("LN1G", [128, 8], F32)
    n.LN1B = sb("LN1B", [128, 8], F32)
    n.LN2G = sb("LN2G", [128, 8], F32)
    n.LN2B = sb("LN2B", [128, 8], F32)
    n.VVB = sb("VVB", [128, 520])
    n.QVB = n.AM[:, 0:512]
    n.KVB = n.AM[:, 512:1024]
    n.SUMS = n.AM[0:16, 0:512]
    n.SUMSF = n.TMP32[0:16, 0:512]
    n.OUTB = sb("OUTB", [128, 8], F32)
    n.FC1B = sb("FC1B", [128, 32], F32)
    n.FC2B = sb("FC2B", [128, 8], F32)

    n.FFB = sb("FFB", [16, 16])
    n.WM = n.QS[:, 4096:12288]
    n.FUTS = n.QS[0:16, 0:4096]
    n.FT1 = n.QS[0:16, 4096:8192]
    n.FVB = n.OT[0:16, 0:1024]

    PSN = ["P0", "P1", "P2", "P3", "P4", "P6", "P7"]
    n.PS = {nm: es.enter_context(nc.psum_tensor(nm, [128, 512], F32)) for nm in PSN}
    n.P0, n.P1, n.P2, n.P3, n.P4, n.P6, n.P7 = (n.PS[nm] for nm in PSN)
    n.P5T = es.enter_context(nc.psum_tensor("P5T", [128, 1024], BF16))

    k.block = es.enter_context(nc.Block())
    with nc.named_scope("init"):
        init(n)
    if "gla" in phases:
        with nc.named_scope("gla"):
            gla(n)
    if "fprep" in phases:
        with nc.named_scope("fprep"):
            fprep(n)
    if "ffn1" in phases:
        last = "ffn1" == [p for p in ("gla", "ffn1", "attnb", "ffn2") if p in phases][-1]
        with nc.named_scope("ffn1"):
            ffn(n, n.xfp.ap(), "xfp", n.x_out if last else n.xfp.ap(), "x_out" if last else "xfp", tag="f1")
    if "attnb" in phases or "attnbC" in phases:
        last = "attnb" in [p for p in ("gla", "ffn1", "attnb", "ffn2") if p in phases][-1:] or "attnbC" in phases
        attnb(n, n.x_out if last else n.xfp.ap(), "x_out" if last else "xfp",
              stop="cache" if "attnbC" in phases else None)
    if "ffn2" in phases:
        with nc.named_scope("ffn2"):
            ffn(n, n.xfp.ap(), "xfp", n.x_out, "x_out", tag="f2")

    emit_debug(n, dbg_names)
    fin(n)
    return nc


def init(n):
    k, op = n.k, n.k.op
    xv = n.x_in.rearrange("(c p) t -> c p t", c=8).transpose([1, 0, 2])
    op("gpsimd", ["x_in"], ["XBF"],
       lambda g: g.dma_start(out=n.XBF[:, 0:8 * T].rearrange("p (c t) -> p c t", c=8), in_=xv),
       chan=k.chan("xb"))
    op("gpsimd", [], ["NDELTA"], lambda g: g.dma_start(out=n.NDELTA[:], in_=n.adelta[:, :]), chan=k.chan("cn"))
    op("gpsimd", [], ["NW"], lambda g: g.dma_start(out=n.NW[:], in_=n.anw[:, :]), chan=k.chan("cn2"))
    op("gpsimd", [], ["ROLEF"], lambda g: g.dma_start(out=n.ROLEF[:], in_=n.rolef_in[:, :]), chan=k.chan("cn3"))
    pass
    op("gpsimd", [], ["ONESB"], lambda g: g.memset(n.ONESB[:], 1.0))
    op("gpsimd", [], ["ONES16"], lambda g: g.memset(n.ONES16[:], 1.0))
    op("gpsimd", [], ["ONESF"], lambda g: g.memset(n.ONESF[:], 1.0))
    op("gpsimd", [], ["SST"], lambda g: g.memset(n.SST[:], 0.0))
    op("gpsimd", [], ["IDENT0"], lambda g: g.memset(n.IDENT[:], 0.0))
    op("gpsimd", [], ["TRIL0"], lambda g: g.memset(n.TRIL[:], 1.0))
    op("gpsimd", [], ["SELB0"], lambda g: g.memset(n.SELB[:], 1.0))
    op("gpsimd", [], ["SELF0"], lambda g: g.memset(n.SELF_[:], 1.0))
    op("gpsimd", ["IDENT0"], ["IDENT"],
       lambda g: g.affine_select(out=n.IDENT[:], in_=n.IDENT[:], compare_op=ALU.not_equal,
                                 fill=1.0, base=0, pattern=[[-1, 128]], channel_multiplier=1))
    op("gpsimd", ["TRIL0"], ["TRIL"],
       lambda g: g.affine_select(out=n.TRIL[:], in_=n.TRIL[:], compare_op=ALU.is_ge,
                                 fill=0.0, base=0, pattern=[[1, 128]], channel_multiplier=-1))
    op("gpsimd", ["SELB0"], ["SELB"],
       lambda g: g.affine_select(out=n.SELB[:], in_=n.SELB[:], compare_op=ALU.is_equal,
                                 fill=0.0, base=0, pattern=[[-1, 16], [0, 64]], channel_multiplier=1))
    op("gpsimd", ["SELF0"], ["SELF"],
       lambda g: g.affine_select(out=n.SELF_[:], in_=n.SELF_[:], compare_op=ALU.is_equal,
                                 fill=0.0, base=0, pattern=[[-1, 16], [0, 64]], channel_multiplier=1))
    op("vector", ["NDELTA"], ["NDELTA"], lambda v: v.tensor_scalar_mul(n.NDELTA[:], n.NDELTA[:], -1.0))
    op("gpsimd", [], ["GATES"], lambda g: g.dma_start(out=n.GATES[:, :], in_=n.fgate[:, :]), chan=k.chan("cn6"))
    op("gpsimd", [], ["FFB"], lambda g: g.dma_start(out=n.FFB[:, :], in_=n.ffb[:, :]), chan=k.chan("cn5"))
    pass
    op("tensor", ["ONESF", "GATES"], ["P0"],
       lambda t: t.matmul(n.P0[:, 0:1], n.ONESF[0:1, :], n.GATES[0:1, 0:1], start=True, stop=True))
    op("vector", ["P0"], ["GATEB"], lambda v: v.tensor_copy(n.GATEB[:, :], n.P0[:, 0:1]))
    op("gpsimd", [], ["ROLEI"], lambda g: g.dma_start(out=n.ROLEI[:, :], in_=n.rolei_in[:, :]), chan=k.chan("cn8"))
    for ci, nm in enumerate("qkvo"):
        op("gpsimd", ["aW" + nm], ["aWb" + nm],
           lambda g, nm=nm: g.dma_start(out=n.aWb[nm].ap(), in_=n.aW[nm]),
           chan=k.chan("cv%d" % (ci % 2)))
    op("gpsimd", ["bfc2Ud"], ["fc2Ub"],
       lambda g: g.dma_start(out=n.fc2Ub.ap(), in_=n.bfc2U), chan=k.chan("cv0"))
    op("gpsimd", [], ["ROWSEL"], lambda g: g.dma_start(out=n.ROWSEL[:, :], in_=n.rowsel_in[:, :]), chan=k.chan("cn8"))
    for nm, dst, res in (("bln1g", n.LN1G, "LN1G"), ("bln1b", n.LN1B, "LN1B"),
                         ("bln2g", n.LN2G, "LN2G"), ("bln2b", n.LN2B, "LN2B"),
                         ("bvVb", n.VVB, "VVB"),
                         ("boutb", n.OUTB, "OUTB"), ("bfc1b", n.FC1B, "FC1B"),
                         ("bfc2b", n.FC2B, "FC2B")):
        op("gpsimd", [], [res], lambda g, nm=nm, dst=dst: g.dma_start(
            out=dst[:, :], in_=getattr(n, nm)[:, :]), chan=k.chan("cn9"))


def ln_full(n, eps=1e-5, srct=None, srcn="XBF"):
    k, op = n.k, n.k.op
    if srct is None:
        srct = n.XBF
    for tt in range(NT):
        t0 = tt * 512

        def sq(v, t0=t0):
            for c in range(8):
                i = v.tensor_mul(n.XH[:, c * 512:(c + 1) * 512],
                                 srct[:, c * T + t0:c * T + t0 + 512],
                                 srct[:, c * T + t0:c * T + t0 + 512])
            return i
        op("vector", [srcn], ["XH"], sq)

        def sums(t, t0=t0):
            for c in range(8):
                t.matmul(n.P0[0:1, :], n.ONESB[:], srct[:, c * T + t0:c * T + t0 + 512],
                         start=(c == 0), stop=(c == 7))
            for c in range(8):
                i = t.matmul(n.P1[0:1, :], n.ONESB[:], n.XH[:, c * 512:(c + 1) * 512],
                             start=(c == 0), stop=(c == 7))
            return i
        op("tensor", [srcn, "XH", "ONESB"], ["P0", "P1"], sums)
        op("vector", ["P0"], ["ROWA"],
           lambda v, t0=t0: v.tensor_scalar_mul(n.ROWA[0:1, t0:t0 + 512], n.P0[0:1, :], 1.0 / 1024.0))
        op("vector", ["P1"], ["SCR2"],
           lambda v: v.tensor_scalar_mul(n.SCR2[0:1, :], n.P1[0:1, :], 1.0 / 1024.0))
        op("vector", ["ROWA"], ["SCR1"],
           lambda v, t0=t0: v.tensor_mul(n.SCR1[0:1, :], n.ROWA[0:1, t0:t0 + 512], n.ROWA[0:1, t0:t0 + 512]))
        op("vector", ["SCR1", "SCR2"], ["SCR1"],
           lambda v: v.tensor_sub(n.SCR1[0:1, :], n.SCR2[0:1, :], n.SCR1[0:1, :]))
        op("vector", ["SCR1"], ["SCR1"],
           lambda v: v.tensor_scalar_add(n.SCR1[0:1, :], n.SCR1[0:1, :], eps))
        op("vector", ["SCR1"], ["SCR2"], lambda v: v.reciprocal(n.SCR2[0:1, :], n.SCR1[0:1, :]))
        op("scalar", ["SCR2"], ["ROWB"],
           lambda s, t0=t0: s.activation(out=n.ROWB[0:1, t0:t0 + 512], in_=n.SCR2[0:1, :], func=AF.Sqrt))


def ln_apply(n, t0, srct=None, srcn="XBF", gt=None, gn=None, bt=None, bn=None):
    k, op = n.k, n.k.op
    if srct is None:
        srct = n.XBF

    def bc(t):
        t.matmul(n.P6[:, :], n.ONESF[0:1, :], n.ROWB[0:1, t0:t0 + 512], start=True, stop=True)
        return t.matmul(n.P7[:, :], n.ONESF[0:1, :], n.ROWA[0:1, t0:t0 + 512], start=True, stop=True)
    op("tensor", ["ROWA", "ROWB", "ONESF"], ["P6", "P7"], bc)
    for c in range(8):
        op("vector", [srcn, "P7"], ["TMP32"],
           lambda v, c=c: v.tensor_sub(n.TMP32[:], srct[:, c * T + t0:c * T + t0 + 512], n.P7[:, :]))
        op("vector", ["TMP32", "P6"], ["XH"],
           lambda v, c=c: v.tensor_mul(n.XH[:, c * 512:(c + 1) * 512], n.TMP32[:], n.P6[:, :]))
        if gt is not None:
            op("vector", ["XH", gn, bn], ["XH"],
               lambda v, c=c: v.tensor_scalar(n.XH[:, c * 512:(c + 1) * 512],
                                              n.XH[:, c * 512:(c + 1) * 512],
                                              gt[:, c:c + 1], bt[:, c:c + 1],
                                              ALU.mult, ALU.add))


def res_write(n, src_dram, src_name, dst_dram, dst_name, fc, t0, idx, bank, bankname):
    k, op = n.k, n.k.op
    r = idx % 2
    XT = n.XT32A if r == 0 else n.XT32B
    XO = n.XO32A if r == 0 else n.XO32B
    xtn = "XT32%s" % ("A" if r == 0 else "B")
    xon = "XO32%s" % ("A" if r == 0 else "B")
    op("sync", [src_name], [xtn],
       lambda s: s.dma_start(out=XT[:], in_=src_dram[fc * 128:(fc + 1) * 128, t0:t0 + 512]),
       chan=k.chan("xt%d" % r))
    op("vector", [bankname, xtn], [xon],
       lambda v: v.tensor_add(XO[:], bank[:, :], XT[:]))
    op("vector", [xon], ["XBF"],
       lambda v: v.tensor_copy(n.XBF[:, fc * T + t0:fc * T + t0 + 512], XO[:]))
    op("sync", [xon], [dst_name],
       lambda s: s.dma_start(out=dst_dram[fc * 128:(fc + 1) * 128, t0:t0 + 512], in_=XO[:]),
       chan=k.chan("xo%d" % r))


def gla(n):
    k, op = n.k, n.k.op
    nc = n.nc
    QS, OT, KD, XH, XBF = n.QS, n.OT, n.KD, n.XH, n.XBF
    P0, P1, P2, P3, P4, P6, P7, P5T = n.P0, n.P1, n.P2, n.P3, n.P4, n.P6, n.P7, n.P5T

    k.wait_all("vector", ["SST"])
    ln_full(n, 1e-5)

    for g in range(NT):
        t0 = g * 512
        ln_apply(n, t0)

        op("sync", ["aWbq"], ["W0"],
           lambda g2: g2.dma_start(out=n.W0.rearrange("p (c f) -> p c f", c=8),
                                   in_=n.aWb["q"].ap().rearrange("(c p) f -> c p f", c=8).transpose([1, 0, 2])),
           chan=k.chan("gw0"))
        op("sync", ["aWbk"], ["W1"],
           lambda g2: g2.dma_start(out=n.W1.rearrange("p (c f) -> p c f", c=8),
                                   in_=n.aWb["k"].ap().rearrange("(c p) f -> c p f", c=8).transpose([1, 0, 2])),
           chan=k.chan("gw1"))
        if g == 0:
            op("gpsimd", ["aWf"], ["WF"],
               lambda g2: g2.dma_start(out=n.WF.rearrange("p (c f) -> p c f", c=8),
                                       in_=n.aWf.rearrange("(c p) f -> c p f", c=8).transpose([1, 0, 2])),
               chan=k.chan("wf"))

        def proj(W, Wname, dst_fn, dstname):
            for fc in range(8):
                bn = ["P0", "P1", "P2", "P3"][fc % 4]
                bank = n.PS[bn]

                def mmf(t, fc=fc, bank=bank, W=W):
                    for dc in range(8):
                        i = t.matmul(bank[:, :],
                                     W[:, dc * 1024 + fc * 128:dc * 1024 + fc * 128 + 128],
                                     XH[:, dc * 512:(dc + 1) * 512],
                                     start=(dc == 0), stop=(dc == 7))
                    return i
                op("tensor", [Wname, "XH"], [bn], mmf)
                op("scalar", [bn], [dstname],
                   lambda s, fc=fc, bank=bank: s.activation(out=dst_fn(fc), in_=bank[:, :], func=AF.Silu))

        proj(n.W0, "W0", lambda fc: QS[:, fc * T + t0:fc * T + t0 + 512], "QS")
        op("sync", ["aWbv"], ["W0"],
           lambda g2: g2.dma_start(out=n.W0.rearrange("p (c f) -> p c f", c=8),
                                   in_=n.aWb["v"].ap().rearrange("(c p) f -> c p f", c=8).transpose([1, 0, 2])),
           chan=k.chan("gw0"))
        proj(n.W1, "W1", lambda fc: KD[:, fc * 512:(fc + 1) * 512], "KD")

        def fmm(t):
            for dc in range(8):
                i = t.matmul(P6[0:16, :], n.WF[:, dc * 16:(dc + 1) * 16],
                             XH[:, dc * 512:(dc + 1) * 512], start=(dc == 0), stop=(dc == 7))
            return i
        op("tensor", ["WF", "XH"], ["P6"], fmm)
        op("scalar", ["P6", "NDELTA"], ["TPOS"],
           lambda s: s.activation(out=n.TPOS[:, :], in_=P6[0:16, :], func=AF.Exp, scale=-1.0, bias=n.NDELTA[:, :]))
        op("vector", ["TPOS"], ["TPOS"], lambda v: v.tensor_scalar_add(n.TPOS[:, :], n.TPOS[:, :], 1.0))
        op("scalar", ["TPOS"], ["TPOS"], lambda s: s.activation(out=n.TPOS[:, :], in_=n.TPOS[:, :], func=AF.Ln))

        for lc in range(LCH):
            co = lc * 128

            def vmm1(t, co=co):
                for dc in range(8):
                    i = t.matmul(P4[:, :], XH[:, dc * 512 + co:dc * 512 + co + 128],
                                 n.W0[:, dc * 1024:dc * 1024 + 512], start=(dc == 0), stop=(dc == 7))
                return i
            op("tensor", ["W0", "XH"], ["P4"], vmm1)
            op("vector", ["P4"], ["VTOK.c%d" % lc],
               lambda v, lc=lc: v.tensor_copy(n.VTOK[:, lc * 1024:lc * 1024 + 512], P4[:, :]))

            def vmm2(t, co=co):
                for dc in range(8):
                    i = t.matmul(P4[:, :], XH[:, dc * 512 + co:dc * 512 + co + 128],
                                 n.W0[:, dc * 1024 + 512:dc * 1024 + 1024], start=(dc == 0), stop=(dc == 7))
                return i
            op("tensor", ["W0", "XH"], ["P4"], vmm2)
            op("vector", ["P4"], ["VTOK.c%d" % lc],
               lambda v, lc=lc: v.tensor_copy(n.VTOK[:, lc * 1024 + 512:lc * 1024 + 1024], P4[:, :]))

        for lc in range(LCH):
            op("vector", ["TPOS", "ONES16"], ["CUM"],
               lambda v, lc=lc: v.tensor_tensor_scan(n.CUM[:, lc * 128:(lc + 1) * 128],
                                                     n.TPOS[:, lc * 128:(lc + 1) * 128],
                                                     n.ONES16[:, :], 0.0, ALU.add, ALU.mult))
        op("vector", ["CUM"], ["CLALL"],
           lambda v, g=g: v.tensor_copy(n.CLALL[:, g * 4:(g + 1) * 4], n.CUM[:, 127::128]))
        op("vector", ["CLALL"], ["NEGCL"],
           lambda v, g=g: v.tensor_scalar_mul(n.NEGCL[:, g * 4:(g + 1) * 4], n.CLALL[:, g * 4:(g + 1) * 4], -1.0))
        op("scalar", ["CUM"], ["EQ"],
           lambda s: s.activation(out=n.EQ[:, :], in_=n.CUM[:, :], func=AF.Exp, scale=-1.0))
        op("scalar", ["CUM"], ["EK"],
           lambda s: s.activation(out=n.EK[:, :], in_=n.CUM[:, :], func=AF.Exp))
        op("scalar", ["NEGCL"], ["DD"],
           lambda s, g=g: s.activation(out=n.DD[:, g * 4:(g + 1) * 4], in_=n.NEGCL[:, g * 4:(g + 1) * 4], func=AF.Exp))

        def ddbmm(t, g=g):
            for pr in range(8):
                i = t.matmul(P7[:, pr * 4:(pr + 1) * 4], n.SELF_[:, pr * 128:(pr + 1) * 128],
                             n.DD[:, g * 4:(g + 1) * 4], start=True, stop=True)
            return i
        op("tensor", ["SELF", "DD"], ["P7"], ddbmm)
        op("vector", ["P7"], ["DDB"], lambda v: v.tensor_copy(n.DDB[:, :], P7[:, 0:32]))

        for pr in range(8):
            for lc in range(LCH):
                def ebc(t, pr=pr, lc=lc):
                    t.matmul(P4[:, 0:128], n.SELB[:, pr * 128:(pr + 1) * 128],
                             n.EQ[:, lc * 128:(lc + 1) * 128], start=True, stop=True)
                    return t.matmul(P4[:, 128:256], n.SELB[:, pr * 128:(pr + 1) * 128],
                                    n.EK[:, lc * 128:(lc + 1) * 128], start=True, stop=True)
                op("tensor", ["SELB", "EQ", "EK"], ["P4"], ebc)

                def qskd(v, pr=pr, lc=lc):
                    qsl = QS[:, pr * T + t0 + lc * 128: pr * T + t0 + lc * 128 + 128]
                    kdl = KD[:, pr * 512 + lc * 128: pr * 512 + lc * 128 + 128]
                    v.tensor_mul(qsl, qsl, P4[:, 0:128])
                    return v.tensor_mul(kdl, kdl, P4[:, 128:256])
                op("vector", ["P4", "QS", "KD"], ["QS", "KD"], qskd)

            def tps(t, pr=pr):
                for lc in range(LCH):
                    i = t.transpose(P5T[:, lc * 128:(lc + 1) * 128],
                                    KD[:, pr * 512 + lc * 128: pr * 512 + lc * 128 + 128], n.IDENT[:])
                return i
            op("tensor", ["KD", "IDENT"], ["P5T"], tps)
            op("vector", ["P5T"], ["XH"],
               lambda v, pr=pr: v.tensor_copy(XH[:, pr * 512:(pr + 1) * 512], P5T[:, 0:512]))

        def emit_at(pr, lc):
            b = pr % 2
            cs_ = 256 + b * 128

            def at(t, pr=pr, lc=lc, cs_=cs_):
                qsl = QS[:, pr * T + t0 + lc * 128: pr * T + t0 + lc * 128 + 128]
                kdl = KD[:, pr * 512 + lc * 128: pr * 512 + lc * 128 + 128]
                t.matmul(P4[:, cs_:cs_ + 128], kdl[0:64, :], qsl[0:64, :], start=True, stop=True)
                return t.matmul(P7[:, cs_:cs_ + 128], kdl[64:128, :], qsl[64:128, :], start=True, stop=True,
                                tile_position=(64, 0))
            op("tensor", ["KD", "QS"], ["P4", "P7"], at)

        emit_at(0, 0)
        for lc in range(LCH):
            for pr in range(8):
                if pr + 1 < 8:
                    emit_at(pr + 1, lc)
                elif lc + 1 < LCH:
                    emit_at(0, lc + 1)
                b = pr % 2
                cs_ = 256 + b * 128
                B6, b6n = (P6, "P6") if b == 0 else (P0, "P0")
                amn = "AM%d" % b

                def msk(v, b=b, cs_=cs_):
                    v.tensor_mul(n.AM[:, b * 128:b * 128 + 128], P4[:, cs_:cs_ + 128], n.TRIL[:])
                    return v.tensor_mul(n.AM[:, 512 + b * 128:512 + b * 128 + 128],
                                        P7[:, cs_:cs_ + 128], n.TRIL[:])
                op("vector", ["P4", "P7", "TRIL"], [amn], msk)

                def ou(t, pr=pr, lc=lc, b=b, B6=B6):
                    vt = n.VTOK[:, lc * 1024 + pr * 128: lc * 1024 + pr * 128 + 128]
                    kt = XH[:, pr * 512 + lc * 128:pr * 512 + lc * 128 + 128]
                    t.matmul(B6[0:64, 0:128], vt[:, 0:64], n.AM[:, b * 128:b * 128 + 128],
                             start=True, stop=True)
                    t.matmul(B6[64:128, 0:128], vt[:, 64:128],
                             n.AM[:, 512 + b * 128:512 + b * 128 + 128],
                             start=True, stop=True, tile_position=(0, 64))
                    t.matmul(B6[0:64, 256:320], kt[:, 0:64], vt[:, 0:64], start=True, stop=True)
                    return t.matmul(B6[64:128, 256:320], kt[:, 64:128], vt[:, 64:128],
                                    start=True, stop=True, tile_position=(0, 64))
                op("tensor", ["VTOK.c%d" % lc, "XH", amn], [b6n], ou)

                u = pr % 4
                op("vector", [b6n], ["OT"],
                   lambda v, pr=pr, lc=lc, B6=B6: v.tensor_copy(
                       OT[:, pr * T + t0 + lc * 128: pr * T + t0 + lc * 128 + 128], B6[:, 0:128]))
                op("vector", [b6n], ["USB%d" % u],
                   lambda v, u=u, B6=B6: v.tensor_copy(n.USB[:, u * 64:(u + 1) * 64], B6[:, 256:320]))
                op("vector", ["SSTp%d" % pr], ["SPREB%d" % u],
                   lambda v, pr=pr, u=u: v.tensor_copy(n.SPREB[:, u * 64:(u + 1) * 64],
                                                       n.SST[:, pr * 64:(pr + 1) * 64]))
                op("vector", ["SSTp%d" % pr, "USB%d" % u], ["SSTp%d" % pr],
                   lambda v, pr=pr, u=u: v.tensor_add(n.SST[:, pr * 64:(pr + 1) * 64],
                                                      n.SST[:, pr * 64:(pr + 1) * 64],
                                                      n.USB[:, u * 64:(u + 1) * 64]))
                op("vector", ["SSTp%d" % pr, "DDB"], ["SSTp%d" % pr],
                   lambda v, pr=pr, lc=lc: v.tensor_scalar(n.SST[:, pr * 64:(pr + 1) * 64],
                                                           n.SST[:, pr * 64:(pr + 1) * 64],
                                                           n.DDB[:, pr * 4 + lc:pr * 4 + lc + 1], None, ALU.mult))

                def oi(t, pr=pr, lc=lc, u=u, B6=B6):
                    sp = n.SPREB[:, u * 64:(u + 1) * 64]
                    qsl = QS[:, pr * T + t0 + lc * 128: pr * T + t0 + lc * 128 + 128]
                    t.matmul(B6[0:64, 128:256], sp[0:64, :], qsl[0:64, :], start=True, stop=True)
                    return t.matmul(B6[64:128, 128:256], sp[64:128, :], qsl[64:128, :], start=True, stop=True,
                                    tile_position=(64, 64))
                op("tensor", ["SPREB%d" % u, "QS"], [b6n], oi)
                op("vector", [b6n, "OT"], ["OT"],
                   lambda v, pr=pr, lc=lc, B6=B6: v.tensor_add(
                       OT[:, pr * T + t0 + lc * 128: pr * T + t0 + lc * 128 + 128],
                       OT[:, pr * T + t0 + lc * 128: pr * T + t0 + lc * 128 + 128], B6[:, 128:256]))

    op("vector", ["CLALL"], ["CLS"],
       lambda v: v.reduce_sum(n.CLS[:, :], n.CLALL[:, :], axis=mybir.AxisListType.X))
    op("scalar", ["CLS"], ["CLS"],
       lambda s: s.activation(out=n.CLS[:, :], in_=n.CLS[:, :], func=AF.Exp, scale=-1.0))
    for pr in range(8):
        op("gpsimd", ["SSTp%d" % pr], ["gla_in"],
           lambda g2, pr=pr: g2.dma_start(out=n.gla_in.ap()[pr * 128:(pr + 1) * 128, :],
                                          in_=n.SST[:, pr * 64:(pr + 1) * 64]), chan=k.chan("ga"))
    op("vector", [], ["CP16"], lambda v: v.memset(n.CP16[:, :], 0.0))
    op("vector", ["CLS", "CP16"], ["CP16"], lambda v: v.tensor_copy(n.CP16[:, 0:1], n.CLS[:, :]))
    op("gpsimd", ["CP16"], ["gla_in"],
       lambda g2: g2.dma_start(out=n.gla_in.ap()[1024:1040, :], in_=n.CP16[:, :]), chan=k.chan("ga"))

    st_gla = k.state.get("gla_in")

    def coll_body(g2):
        if st_gla and st_gla["w"]:
            k._wait(g2, "gpsimd", *st_gla["w"])
        i = g2.collective_compute("AllGather", ALU.bypass,
                                  replica_groups=[[0, 1], [2, 3], [4, 5], [6, 7]],
                                  ins=[n.gla_in.ap().opt()], outs=[n.gla_out.ap().opt()])
        k.cc.inc(i, 1)
    k.block.gpsimd(coll_body)
    k.state["gla_out"] = {"w": (k.cc, k.cc.n), "r": {}}

    for pr in range(8):
        op("gpsimd", ["gla_out"], ["PST"],
           lambda g2, pr=pr: g2.dma_start(out=n.PST[:, pr * 64:(pr + 1) * 64],
                                          in_=n.gla_out.ap()[0, pr * 128:(pr + 1) * 128, :]), chan=k.chan("ps"))
    op("vector", ["PST", "ROLEF"], ["PST"],
       lambda v: v.tensor_scalar(n.PST[:, :], n.PST[:, :], n.ROLEF[:, :], None, ALU.mult))
    op("vector", ["CLALL", "ONES16"], ["INCL"],
       lambda v: v.tensor_tensor_scan(n.INCL[:, :], n.CLALL[:, :], n.ONES16[:, 0:16], 0.0, ALU.add, ALU.mult))
    op("vector", ["CLALL", "INCL"], ["INCL"],
       lambda v: v.tensor_sub(n.INCL[:, :], n.CLALL[:, :], n.INCL[:, :]))
    op("scalar", ["INCL"], ["PREFE"],
       lambda s: s.activation(out=n.PREFE[:, :], in_=n.INCL[:, :], func=AF.Exp))

    def prefbmm(t):
        for pr in range(8):
            i = t.matmul(P7[:, pr * 16:(pr + 1) * 16], n.SELF_[:, pr * 128:(pr + 1) * 128],
                         n.PREFE[:, :], start=True, stop=True)
        return i
    op("tensor", ["SELF", "PREFE"], ["P7"], prefbmm)
    op("vector", ["P7"], ["PREFB"], lambda v: v.tensor_copy(n.PREFB[:, :], P7[:, 0:128]))

    for pr in range(8):
        for ch in range(16):
            buf = ch % 2
            op("vector", ["PST", "PREFB"], ["PCB%d" % buf],
               lambda v, pr=pr, ch=ch, buf=buf: v.tensor_scalar(
                   n.PCB[:, buf * 64:(buf + 1) * 64], n.PST[:, pr * 64:(pr + 1) * 64],
                   n.PREFB[:, pr * 16 + ch:pr * 16 + ch + 1], None, ALU.mult))

            def cmm(t, pr=pr, ch=ch, buf=buf):
                sp = n.PCB[:, buf * 64:(buf + 1) * 64]
                qsl = QS[:, pr * T + ch * 128: pr * T + ch * 128 + 128]
                t.matmul(P6[0:64, 128:256], sp[0:64, :], qsl[0:64, :], start=True, stop=True)
                return t.matmul(P6[64:128, 128:256], sp[64:128, :], qsl[64:128, :], start=True, stop=True,
                                tile_position=(64, 64))
            op("tensor", ["PCB%d" % buf, "QS"], ["P6"], cmm)
            op("vector", ["P6", "OT"], ["OT"],
               lambda v, pr=pr, ch=ch: v.tensor_add(
                   OT[:, pr * T + ch * 128: pr * T + ch * 128 + 128],
                   OT[:, pr * T + ch * 128: pr * T + ch * 128 + 128], P6[:, 128:256]))

    for tt in range(NT):
        t0 = tt * 512

        def sq2(v, t0=t0):
            for pr in range(8):
                i = v.tensor_mul(XH[:, pr * 512:(pr + 1) * 512],
                                 OT[:, pr * T + t0:pr * T + t0 + 512],
                                 OT[:, pr * T + t0:pr * T + t0 + 512])
            return i
        op("vector", ["OT"], ["XH"], sq2)

        def sums2(t):
            for pr in range(8):
                i = t.matmul(P0[0:1, :], n.ONESB[:], XH[:, pr * 512:(pr + 1) * 512],
                             start=(pr == 0), stop=(pr == 7))
            return i
        op("tensor", ["XH", "ONESB"], ["P0"], sums2)
        op("vector", ["P0"], ["SCR1"], lambda v: v.tensor_scalar_mul(n.SCR1[0:1, :], P0[0:1, :], 1.0 / 1024.0))
        op("vector", ["SCR1"], ["SCR1"], lambda v: v.tensor_scalar_add(n.SCR1[0:1, :], n.SCR1[0:1, :], 1e-6))
        op("vector", ["SCR1"], ["SCR2"], lambda v: v.reciprocal(n.SCR2[0:1, :], n.SCR1[0:1, :]))
        op("scalar", ["SCR2"], ["ROWB"],
           lambda s, t0=t0: s.activation(out=n.ROWB[0:1, t0:t0 + 512], in_=n.SCR2[0:1, :], func=AF.Sqrt))

    for tt in range(NT):
        t0 = tt * 512
        op("tensor", ["ROWB", "ONESF"], ["P7"],
           lambda t, t0=t0: t.matmul(P7[:, :], n.ONESF[0:1, :], n.ROWB[0:1, t0:t0 + 512], start=True, stop=True))

        def ohat(v, t0=t0):
            for pr in range(8):
                i = v.scalar_tensor_tensor(QS[:, pr * T + t0:pr * T + t0 + 512],
                                           OT[:, pr * T + t0:pr * T + t0 + 512],
                                           n.NW[:, pr:pr + 1], P7[:, :], ALU.mult, ALU.mult)
            return i
        op("vector", ["OT", "NW", "P7", "QS"], ["QS"], ohat)

    op("sync", ["aWbo"], ["W1"],
       lambda g2: g2.dma_start(out=n.W1.rearrange("p (c f) -> p c f", c=8),
                               in_=n.aWb["o"].ap().rearrange("(c p) f -> c p f", c=8).transpose([1, 0, 2])),
       chan=k.chan("gw1"))

    for fc in range(8):
        for tt in range(NT):
            t0 = tt * 512
            bn = ["P0", "P1", "P2", "P3"][(fc * NT + tt) % 4]
            bank = n.PS[bn]

            def womm(t, fc=fc, t0=t0, bank=bank):
                for pr in range(8):
                    i = t.matmul(bank[:, :],
                                 n.W1[:, pr * 1024 + fc * 128:pr * 1024 + fc * 128 + 128],
                                 QS[:, pr * T + t0:pr * T + t0 + 512],
                                 start=(pr == 0), stop=(pr == 7))
                return i
            op("tensor", ["W1", "QS"], [bn], womm)
            res_write(n, n.x_in, "x_in", n.xfp.ap(), "xfp", fc, t0, fc * NT + tt, bank, bn)


def fprep(n):
    k, op = n.k, n.k.op
    P0 = n.P0

    def scw(ft):
        row = n.ROWA if ft < 4 else n.ROWB
        nm = "ROWA" if ft < 4 else "ROWB"
        c0 = (ft % 4) * 512
        return row[0:1, c0:c0 + 512], nm

    op("gpsimd", [], ["QS"], lambda g: g.dma_start(out=n.FUTS, in_=n.fuT[:, :]), chan=k.chan("cn7"))
    op("gpsimd", [], ["OT"], lambda g: g.dma_start(out=n.FVB, in_=n.fv[:, :]), chan=k.chan("cn4"))
    for ft in range(8):
        op("tensor", ["FFB", "QS"], ["P0"],
           lambda t, ft=ft: t.matmul(P0[0:16, :], n.FFB[:, :], n.FUTS[:, ft * 512:(ft + 1) * 512],
                                     start=True, stop=True))
        op("vector", ["P0"], ["QS"],
           lambda v, ft=ft: v.tensor_copy(n.FT1[:, ft * 512:(ft + 1) * 512], P0[0:16, :]))

    strips = [(n.XT32A, "XT32A"), (n.XT32B, "XT32B"), (n.XO32A, "XO32A"), (n.XO32B, "XO32B")]

    for wtag, wsrc, nchunk, nf, dstd in (
            ("fc", n.fwfcT, 8, 4096, n.weff_fc),
            ("pj", n.fwprojT, 32, 1024, n.weff_pj)):
        nft = nf // 512
        recip = 1.0 / (128 * nchunk)
        for ft in range(nft):
            rw, rn = scw(ft)
            op("vector", [], [rn], lambda v, rw=rw: v.memset(rw, 0.0))
        si = 0
        for c in range(nchunk):
            for ft in range(nft):
                sl = slice(ft * 512, (ft + 1) * 512)
                S, sn = strips[si % 4]
                si += 1
                op("sync", [], [sn],
                   lambda s2, c=c, sl=sl, S=S: s2.dma_start(out=S[:], in_=wsrc[c * 128:(c + 1) * 128, sl]),
                   chan=k.chan("wp%d" % (si % 2)))
                op("scalar", [sn], [sn],
                   lambda s, S=S: s.activation(out=S[:], in_=S[:], func=AF.Abs))
                rw, rn = scw(ft)
                op("tensor", [sn, "ONESF"], ["P0"],
                   lambda t, S=S: t.matmul(P0[0:1, :], n.ONESF[:, 0:1], S[:], start=True, stop=True))
                op("vector", ["P0", rn], [rn], lambda v, rw=rw: v.tensor_add(rw, rw, P0[0:1, :]))
        for ft in range(nft):
            rw, rn = scw(ft)
            op("vector", [rn], [rn], lambda v, rw=rw: v.tensor_scalar_mul(rw, rw, recip))
            op("vector", [rn], [rn], lambda v, rw=rw: v.tensor_scalar_max(rw, rw, 1e-5))
        for c in range(nchunk):
            for ft in range(nft):
                sl = slice(ft * 512, (ft + 1) * 512)
                S, sn = strips[si % 4]
                slot = si % 4
                karena = n.KD if slot < 2 else n.VTOK
                kbase = (slot % 2) * 1536
                kgt = karena[:, kbase:kbase + 512]
                ksg = karena[:, kbase + 512:kbase + 1024]
                kout = karena[:, kbase + 1024:kbase + 1536]
                kn = "KDf%d" % slot
                bsc, bscn = (n.P6, "P6") if slot % 2 == 0 else (n.P1, "P1")
                blr, blrn = (n.P7, "P7") if slot % 2 == 0 else (n.P2, "P2")
                si += 1
                op("sync", [], [sn],
                   lambda s2, c=c, sl=sl, S=S: s2.dma_start(out=S[:], in_=wsrc[c * 128:(c + 1) * 128, sl]),
                   chan=k.chan("wp%d" % (si % 2)))
                rw, rn = scw(ft)
                op("tensor", [rn, "ONESF"], [bscn],
                   lambda t, rw=rw, bsc=bsc: t.matmul(bsc[:, :], n.ONESF[0:1, :], rw, start=True, stop=True))
                op("scalar", [sn, kn], [kn],
                   lambda s, S=S, ksg=ksg: s.activation(out=ksg, in_=S[:], func=AF.Sign))
                op("scalar", [sn], [sn],
                   lambda s, S=S: s.activation(out=S[:], in_=S[:], func=AF.Abs, scale=2.0))
                op("vector", [sn, bscn, kn], [kn],
                   lambda v, S=S, kgt=kgt, bsc=bsc: v.tensor_tensor(kgt, S[:], bsc[:, :], ALU.is_gt))
                op("vector", [bscn, kn], [kn],
                   lambda v, kgt=kgt, bsc=bsc: v.tensor_mul(kgt, kgt, bsc[:, :]))
                op("vector", [kn], [kn],
                   lambda v, kout=kout, kgt=kgt, ksg=ksg: v.tensor_mul(kout, kgt, ksg))
                if wtag == "fc":
                    op("tensor", ["OT", "QS"], [blrn],
                       lambda t, c=c, sl=sl, blr=blr: t.matmul(blr[:, :], n.FVB[:, c * 128:(c + 1) * 128],
                                                               n.FT1[:, sl], start=True, stop=True))
                    op("vector", [blrn, "GATEB", kn], [kn],
                       lambda v, kout=kout, blr=blr: v.scalar_tensor_tensor(kout, blr[:, :], n.GATEB[:, 0:1],
                                                                            kout, ALU.mult, ALU.add))
                op("sync", [kn], ["weff_" + wtag],
                   lambda s2, c=c, sl=sl, dstd=dstd, kout=kout: s2.dma_start(
                       out=dstd.ap()[c * 128:(c + 1) * 128, sl], in_=kout),
                   chan=k.chan("we%d" % (slot % 2)))


def ffn(n, src, srcname, dst, dstname, tag):
    k, op = n.k, n.k.op
    QS = n.QS
    ln_full(n, 1e-5)
    for tt in range(NT):
        t0 = tt * 512
        ln_apply(n, t0)
        for wc in range(4):
            wslot = n.W0 if wc % 2 == 0 else n.W1
            wname = "W0" if wc % 2 == 0 else "W1"
            op("sync", ["weff_fc"], [wname],
               lambda g, wc=wc, wslot=wslot: g.dma_start(
                   out=wslot.rearrange("p (c f) -> p c f", c=8),
                   in_=n.weff_fc.ap()[:, wc * 1024:(wc + 1) * 1024]
                       .rearrange("(c p) f -> c p f", c=8).transpose([1, 0, 2])),
               chan=k.chan("wf" + str(wc % 2)))
            for fcl in range(8):
                fc = wc * 8 + fcl
                bn = ["P0", "P1", "P2", "P3"][fc % 4]
                bank = n.PS[bn]

                def umm(t, fcl=fcl, bank=bank, wslot=wslot):
                    for dc in range(8):
                        i = t.matmul(bank[:, :],
                                     wslot[:, dc * 1024 + fcl * 128:dc * 1024 + fcl * 128 + 128],
                                     n.XH[:, dc * 512:(dc + 1) * 512],
                                     start=(dc == 0), stop=(dc == 7))
                    return i
                op("tensor", [wname, "XH"], [bn], umm)
                op("scalar", [bn], ["QS"],
                   lambda s, fc=fc, bank=bank: s.activation(out=QS[:, fc * 512:(fc + 1) * 512],
                                                            in_=bank[:, :], func=AF.Relu))
                op("vector", ["QS"], ["QS"],
                   lambda v, fc=fc: v.tensor_mul(QS[:, fc * 512:(fc + 1) * 512],
                                                 QS[:, fc * 512:(fc + 1) * 512],
                                                 QS[:, fc * 512:(fc + 1) * 512]))
        for fc in range(8):
            wslot = n.W0 if fc % 2 == 0 else n.W1
            wname = "W0" if fc % 2 == 0 else "W1"
            op("sync", ["weff_pj"], [wname],
               lambda g, fc=fc, wslot=wslot: g.dma_start(
                   out=wslot[:, 0:4096].rearrange("p (c f) -> p c f", c=32),
                   in_=n.weff_pj.ap()[:, fc * 128:(fc + 1) * 128]
                       .rearrange("(c p) f -> c p f", c=32).transpose([1, 0, 2])),
               chan=k.chan("wf" + str(fc % 2)))
            bn = ["P0", "P1", "P2", "P3"][fc % 4]
            bank = n.PS[bn]

            def dmm(t, bank=bank, wslot=wslot):
                for kc in range(32):
                    i = t.matmul(bank[:, :],
                                 wslot[:, kc * 128:kc * 128 + 128],
                                 QS[:, kc * 512:(kc + 1) * 512],
                                 start=(kc == 0), stop=(kc == 31))
                return i
            op("tensor", [wname, "QS"], [bn], dmm)
            res_write(n, src, srcname, dst, dstname, fc, t0, fc, bank, bn)


def attnb(n, dst, dstname, stop=None):
    k, op = n.k, n.k.op
    P0, P1, P2, P3, P4, P6, P7 = n.P0, n.P1, n.P2, n.P3, n.P4, n.P6, n.P7

    k.wait_all("gpsimd", ["AM0", "AM1"])
    for nm, dstt in (("bqVb", n.QVB), ("bkVb", n.KVB)):
        op("gpsimd", [], ["AM"], lambda g, nm=nm, dstt=dstt: g.dma_start(
            out=dstt[:, :], in_=getattr(n, nm)[:, :]), chan=k.chan("cn9"))

    _sid = n.nc.enter_named_scope("ab_proj", False)[0]
    for eng in ("gpsimd", "vector", "scalar", "tensor"):
        k.wait_all(eng, ["KDf0", "KDf1", "KDf2", "KDf3"])
    MP = lambda d: n.VTOK[:, d * 512:(d + 1) * 512]
    MD0 = n.VTOK[:, 4 * 512:5 * 512]
    MD1 = n.VTOK[:, 5 * 512:6 * 512]
    TB = n.VTOK[:, 6 * 512:7 * 512]
    for d in range(4):
        op("gpsimd", [], ["VTOK"], lambda g, d=d: g.memset(MP(d), 1.0))
        op("gpsimd", ["VTOK"], ["VTOK"],
           lambda g, d=d: g.affine_select(out=MP(d), in_=MP(d), compare_op=ALU.is_ge,
                                          fill=0.0, base=-128 * d, pattern=[[1, 512]],
                                          channel_multiplier=-1))
    op("vector", ["ROLEF", "TMP32"], ["TMP32"],
       lambda v: v.tensor_scalar(n.TMP32[:, 0:1], n.ROLEF[:, :], -1.0, 1.0, ALU.mult, ALU.add))
    for (mdst, pa, pb) in ((MD0, MP(0), MP(1)), (MD1, MP(2), MP(3))):
        op("vector", ["VTOK", "ROLEF"], ["VTOK"],
           lambda v, pb=pb: v.tensor_scalar(TB, pb, n.ROLEF[0:128, 0:1], None, ALU.mult))
        op("vector", ["VTOK", "TMP32"], ["VTOK"],
           lambda v, pa=pa, mdst=mdst: v.scalar_tensor_tensor(mdst, pa, n.TMP32[:, 0:1], TB,
                                                              ALU.mult, ALU.add))

    ln_full(n, 1e-5)
    for tt in range(NT):
        t0 = tt * 512
        ln_apply(n, t0)
        for c in range(8):
            op("vector", ["XH"], ["OT"],
               lambda v, c=c, t0=t0: v.tensor_copy(n.OT[:, c * T + t0:c * T + t0 + 512],
                                                   n.XH[:, c * 512:(c + 1) * 512]))
    ln_full(n, 1e-5, srct=n.OT, srcn="OT")
    for tt in range(NT):
        t0 = tt * 512
        ln_apply(n, t0, srct=n.OT, srcn="OT", gt=n.LN1G, gn="LN1G", bt=n.LN1B, bn="LN1B")
        for (wap, wn, agin, agn) in ((n.bqU, "bqUd", n.pq_in, "pq_in"),
                                     (n.bkU, "bkUd", n.pk_in, "pk_in"),
                                     (n.bvU, "bvUd", n.pv_in, "pv_in")):
            op("gpsimd", [wn], ["W0"],
               lambda g, wap=wap: g.dma_start(out=n.W0.rearrange("p (c f) -> p c f", c=8),
                                              in_=wap.rearrange("(c p) f -> c p f", c=8).transpose([1, 0, 2])),
               chan=k.chan("w0"))
            for fc in range(8):
                bn_ = ["P0", "P1", "P2", "P3"][fc % 4]
                bank = n.PS[bn_]

                def pmm(t, fc=fc, bank=bank):
                    for dc in range(8):
                        i = t.matmul(bank[:, :],
                                     n.W0[:, dc * 1024 + fc * 128:dc * 1024 + fc * 128 + 128],
                                     n.XH[:, dc * 512:(dc + 1) * 512],
                                     start=(dc == 0), stop=(dc == 7))
                    return i
                op("tensor", ["W0", "XH"], [bn_], pmm)
                stg = n.KD[:, (fc % 2) * 512:(fc % 2) * 512 + 512]
                op("vector", [bn_], ["KD"],
                   lambda v, bank=bank, stg=stg: v.tensor_copy(stg, bank[:, :]))
                op("vector", ["KD", "ROWSEL"], ["KD"],
                   lambda v, stg=stg: v.tensor_scalar(stg, stg, n.ROWSEL[:, 0:1], None, ALU.add))
                if agn == "pq_in":
                    op("sync", ["KD"], [agn],
                       lambda g, fc=fc, t0=t0, stg=stg, agin=agin: g.dma_start(
                           out=agin.ap()[fc * 128:(fc + 1) * 128, t0:t0 + 512], in_=stg),
                       chan=k.chan("ag%d" % (fc % 2)))
                else:
                    for par in range(2):
                        op("sync", ["KD"], [agn],
                           lambda g, fc=fc, tt=tt, par=par, stg=stg, agin=agin: g.dma_start(
                               out=agin.ap()[fc * 128:(fc + 1) * 128,
                                             par * 1024 + 2 * tt * 128:par * 1024 + (2 * tt + 2) * 128]
                                   .rearrange("p (q w) -> p q w", q=2, w=128),
                               in_=stg.rearrange("p (u par w) -> p par u w", u=2, par=2, w=128)[:, par]),
                           chan=k.chan("ag%d" % (fc % 2)))

    n.nc.leave_named_scope("ab_proj", _sid, False)
    _sid = n.nc.enter_named_scope("ab_coll", False)[0]
    sts = [k.state.get(nm) for nm in ("pq_in", "pk_in", "pv_in")]

    marks = {}

    def coll_body(g2):
        for st in sts:
            if st and st["w"]:
                k._wait(g2, "gpsimd", *st["w"])
        for pin, pout, nm in ((n.pk_in, n.pk_out, "pk_out"), (n.pv_in, n.pv_out, "pv_out"),
                              (n.pq_in, n.pq_out, "pq_out")):
            i = g2.collective_compute("AllGather", ALU.bypass,
                                      replica_groups=[[0, 1], [2, 3], [4, 5], [6, 7]],
                                      ins=[pin.ap().opt()], outs=[pout.ap().opt()])
            marks[nm] = k.cc.inc(i, 1)
    k.block.gpsimd(coll_body)
    for nm in ("pq_out", "pk_out", "pv_out"):
        k.state[nm] = {"w": (k.cc, marks[nm]), "r": {}}


    n.nc.leave_named_scope("ab_coll", _sid, False)
    _sid = n.nc.enter_named_scope("ab_cache", False)[0]
    KVB_B = n.KD[:, 0:512]
    KVB_A = n.KD[:, 512:1024]
    VVB_B = n.KD[:, 1024:1544]
    VVB_A = n.KD[:, 1544:2064]
    op("vector", ["AM", "ROLEF"], ["KD"],
       lambda v: v.tensor_scalar(KVB_B, n.KVB[:, :], n.ROLEF[:, 0:1], None, ALU.mult))
    op("vector", ["AM", "KD"], ["KD"],
       lambda v: v.tensor_sub(KVB_A, n.KVB[:, :], KVB_B))
    op("vector", ["VVB", "ROLEF"], ["KD"],
       lambda v: v.tensor_scalar(VVB_B, n.VVB[:, :], n.ROLEF[:, 0:1], None, ALU.mult))
    op("vector", ["VVB", "KD"], ["KD"],
       lambda v: v.tensor_sub(VVB_A, n.VVB[:, :], VVB_B))

    def par_dma(agout, agn, rank, par, wt, wn):
        op("sync", [agn], [wn],
           lambda g: g.dma_start(
               out=wt.rearrange("p (c q) -> p c q", c=8),
               in_=agout.ap()[rank][:, par * 1024:(par + 1) * 1024]
                   .rearrange("(c p) q -> c p q", c=8).transpose([1, 0, 2])),
           chan=k.chan("kb0" if par == 0 else "kb0b"))

    for rank in range(2):
        par_dma(n.pk_out, "pk_out", rank, 0, n.W0, "W0")
        par_dma(n.pk_out, "pk_out", rank, 1, n.W1, "W1")
        for pr in range(8):
            for h0 in (0, 512):
                def kmm(t, pr=pr, h0=h0):
                    cs = pr * 1024 + h0
                    t.matmul(P4[0:64, 0:512], KVB_A[0:49, pr * 64:(pr + 1) * 64],
                             n.W0[0:49, cs:cs + 512], start=True, stop=False)
                    t.matmul(P4[0:64, 0:512], KVB_B[0:49, pr * 64:(pr + 1) * 64],
                             n.W1[0:49, cs:cs + 512], start=False, stop=True)
                    t.matmul(P4[64:128, 0:512], KVB_A[64:113, pr * 64:(pr + 1) * 64],
                             n.W0[64:113, cs:cs + 512], start=True, stop=False,
                             tile_position=(64, 64))
                    return t.matmul(P4[64:128, 0:512], KVB_B[64:113, pr * 64:(pr + 1) * 64],
                                    n.W1[64:113, cs:cs + 512], start=False, stop=True,
                                    tile_position=(64, 64))
                op("tensor", ["KD", "W0", "W1"], ["P4"], kmm)
                s0 = rank * 8 + h0 // 128
                op("vector", ["P4"], ["QS"],
                   lambda v, pr=pr, s0=s0: v.tensor_copy(
                       n.QS[:, (pr * 16 + s0) * 128:(pr * 16 + s0) * 128 + 512],
                       P4[:, 0:512]))
    for rank in range(2):
        par_dma(n.pv_out, "pv_out", rank, 0, n.W0, "W0")
        par_dma(n.pv_out, "pv_out", rank, 1, n.W1, "W1")
        for s8 in range(8):
            s = rank * 8 + s8
            for h in range(16):
                pr, hb = h // 2, h % 2
                rlo = 64 * hb
                bk = P6 if hb == 0 else P7
                bkn = "P6" if hb == 0 else "P7"

                def vmm(t, s8=s8, pr=pr, rlo=rlo, bk=bk):
                    cs = pr * 1024 + s8 * 128
                    t.matmul(bk[:, 0:65], n.W0[rlo:rlo + 49, cs:cs + 128],
                             VVB_A[rlo:rlo + 49, pr * 65:(pr + 1) * 65],
                             start=True, stop=False)
                    return t.matmul(bk[:, 0:65], n.W1[rlo:rlo + 49, cs:cs + 128],
                                    VVB_B[rlo:rlo + 49, pr * 65:(pr + 1) * 65],
                                    start=False, stop=True)
                op("tensor", ["KD", "W0", "W1"], [bkn], vmm)
                op("vector", [bkn], ["XBF"],
                   lambda v, s=s, h=h, bk=bk: v.tensor_copy(
                       n.XBF[:, (s * 16 + h) * 65:(s * 16 + h) * 65 + 65], bk[:, 0:65]))

    n.nc.leave_named_scope("ab_cache", _sid, False)
    if stop == "cache":
        return
    _sid = n.nc.enter_named_scope("ab_attn", False)[0]
    for eng in ("scalar", "vector", "tensor", "gpsimd"):
        k.wait_all(eng, ["KD", "W0"])
    qt_order = [0, 4, 1, 5, 2, 6, 3, 7]
    for oi_, qt in enumerate(qt_order):
        rank, co = qt // 4, (qt % 4) * 512
        qtn = "QTa" if oi_ % 2 == 0 else "QTb"
        qto = 0 if oi_ % 2 == 0 else 4096
        op("sync", ["pq_out"], ["XH"],
           lambda g, rank=rank, co=co: g.dma_start(
               out=n.XH[:, 0:4096].rearrange("p (c w) -> p c w", c=8),
               in_=n.pq_out.ap()[rank][:, co:co + 512]
                   .rearrange("(c p) w -> c p w", c=8).transpose([1, 0, 2])),
           chan=k.chan("kb1"))
        for pr in range(8):
            def qmm(t, pr=pr):
                t.matmul(P0[0:64, 0:512], n.QVB[0:49, pr * 64:(pr + 1) * 64],
                         n.XH[0:49, pr * 512:(pr + 1) * 512], start=True, stop=True)
                return t.matmul(P0[64:128, 0:512], n.QVB[64:113, pr * 64:(pr + 1) * 64],
                                n.XH[64:113, pr * 512:(pr + 1) * 512], start=True, stop=True,
                                tile_position=(64, 64))
            op("tensor", ["AM", "XH"], ["P0"], qmm)
            op("vector", ["P0"], [qtn],
               lambda v, pr=pr, qto=qto: v.tensor_copy(
                   n.W0[:, qto + pr * 512:qto + (pr + 1) * 512], P0[:, 0:512]))

        nslot = 2 * qt + 2
        j, half = qt % 4, qt // 4

        def emit_qk(pr, sj):
            sa, san = (P4, "P4") if sj % 2 == 0 else (P7, "P7")
            sb, sbn = (P6, "P6") if sj % 2 == 0 else (P1, "P1")

            def qk(t, pr=pr, sj=sj, sa=sa, sb=sb, qto=qto):
                kl = n.QS[:, (pr * 16 + sj) * 128:(pr * 16 + sj) * 128 + 128]
                qtl = n.W0[:, qto + pr * 512:qto + (pr + 1) * 512]
                t.matmul(sa[:, :], kl[0:64, :], qtl[0:64, :], start=True, stop=True)
                return t.matmul(sb[:, :], kl[64:128, :], qtl[64:128, :], start=True, stop=True,
                                tile_position=(64, 0))
            op("tensor", ["QS", qtn], [san, sbn], qk)

        emit_qk(0, 0)
        for pr in range(8):
            for sj in range(nslot):
                if sj + 1 < nslot:
                    emit_qk(pr, sj + 1)
                elif pr + 1 < 8:
                    emit_qk(pr + 1, 0)
                sa, san = (P4, "P4") if sj % 2 == 0 else (P7, "P7")
                sb, sbn = (P6, "P6") if sj % 2 == 0 else (P1, "P1")
                en = "KDe%d" % (sj % 4)
                ea = n.KD[:, (sj % 4) * 1024:(sj % 4) * 1024 + 512]
                eb = n.KD[:, (sj % 4) * 1024 + 512:(sj % 4) * 1024 + 1024]
                op("scalar", [san], [en],
                   lambda s, sa=sa, ea=ea: s.activation(out=ea, in_=sa[:, :], func=AF.Exp, scale=0.125))
                op("scalar", [sbn], [en],
                   lambda s, sb=sb, eb=eb: s.activation(out=eb, in_=sb[:, :], func=AF.Exp, scale=0.125))
                if sj >= nslot - 2:
                    md = MD0 if sj == nslot - 2 else MD1
                    op("vector", [en, "VTOK"], [en],
                       lambda v, ea=ea, md=md: v.tensor_mul(ea, ea, md))
                    op("vector", [en, "VTOK"], [en],
                       lambda v, eb=eb, md=md: v.tensor_mul(eb, eb, md))

                def av(t, pr=pr, sj=sj, ea=ea, eb=eb):
                    va = n.XBF[:, (sj * 16 + 2 * pr) * 65:(sj * 16 + 2 * pr) * 65 + 65]
                    vb = n.XBF[:, (sj * 16 + 2 * pr + 1) * 65:(sj * 16 + 2 * pr + 1) * 65 + 65]
                    t.matmul(P2[0:65, :], va, ea, start=(sj == 0), stop=(sj == nslot - 1))
                    return t.matmul(P3[0:65, :], vb, eb, start=(sj == 0), stop=(sj == nslot - 1))
                op("tensor", ["XBF", en], ["P2", "P3"], av)

            for (cbn, cb, hh) in (("P2", P2, 2 * pr), ("P3", P3, 2 * pr + 1)):
                tn = "TMPc%d" % (hh % 2)
                tslc = n.TMP32[0:65, :] if hh % 2 == 0 else n.XT32A[0:65, :]
                op("vector", [cbn], [tn],
                   lambda v, cb=cb, tslc=tslc: v.tensor_copy(tslc, cb[0:65, :]))
                op("gpsimd", [tn], ["rs_in%d" % j],
                   lambda g, j=j, half=half, hh=hh, tslc=tslc: g.dma_start(
                       out=n.rs_in[j].ap()[half, hh], in_=tslc),
                   chan=k.chan("cs%d" % (pr % 2)))
        if oi_ % 2 == 1:
            st = k.state.get("rs_in%d" % j)

            def rs_body(g2, j=j, st=st):
                if st and st["w"]:
                    k._wait(g2, "gpsimd", *st["w"])
                i = g2.collective_compute("ReduceScatter", ALU.add,
                                          replica_groups=[[0, 1], [2, 3], [4, 5], [6, 7]],
                                          ins=[n.rs_in[j].ap().opt()], outs=[n.rs_out[j].ap().opt()])
                k.cc.inc(i, 1)
            k.block.gpsimd(rs_body)
            k.state["rs_out%d" % j] = {"w": (k.cc, k.cc.n), "r": {}}

    n.nc.leave_named_scope("ab_attn", _sid, False)
    _sid = n.nc.enter_named_scope("ab_outp", False)[0]
    for eng in ("scalar", "vector", "tensor", "gpsimd", "sync"):
        k.wait_all(eng, ["KDe0", "KDe1", "KDe2", "KDe3", "QTa", "QTb", "TMPc0", "TMPc1"])
    op("gpsimd", ["boutUd"], ["W0"],
       lambda g: g.dma_start(out=n.W0[:, 0:6144].rearrange("p (c f) -> p c f", c=8),
                             in_=n.boutU.rearrange("(c p) f -> c p f", c=8).transpose([1, 0, 2])),
       chan=k.chan("w0"))
    op("gpsimd", ["boutVd"], ["W1"],
       lambda g: g.dma_start(out=n.W1[:, 0:6144].rearrange("p (c f) -> p c f", c=6),
                             in_=n.boutV.rearrange("(c p) f -> c p f", c=6).transpose([1, 0, 2])),
       chan=k.chan("w1"))
    for j in range(4):
        t0 = j * 512
        for pr in range(8):
            op("sync", ["rs_out%d" % j], ["XH"],
               lambda g, j=j, pr=pr: g.dma_start(
                   out=n.XH[0:64, pr * 512:(pr + 1) * 512], in_=n.rs_out[j].ap()[2 * pr, 0:64]),
               chan=k.chan("cl0"))
            op("sync", ["rs_out%d" % j], ["XH"],
               lambda g, j=j, pr=pr: g.dma_start(
                   out=n.XH[64:128, pr * 512:(pr + 1) * 512], in_=n.rs_out[j].ap()[2 * pr + 1, 0:64]),
               chan=k.chan("cl1"))
        op("sync", ["rs_out%d" % j], ["AM"],
           lambda g, j=j: g.dma_start(out=n.SUMS[:, :], in_=n.rs_out[j].ap()[:, 64, :]),
           chan=k.chan("cl2"))
        op("vector", ["AM"], ["TMP32"], lambda v: v.tensor_copy(n.SUMSF[:, :], n.SUMS[:, :]))
        op("vector", ["TMP32"], ["TMP32"], lambda v: v.reciprocal(n.SUMSF[:, :], n.SUMSF[:, :]))
        op("vector", ["TMP32"], ["AM"], lambda v: v.tensor_copy(n.SUMS[:, :], n.SUMSF[:, :]))
        for pr in range(8):
            op("tensor", ["SELB", "AM"], ["P4"],
               lambda t, pr=pr: t.matmul(P4[:, :], n.SELB[:, pr * 128:(pr + 1) * 128],
                                         n.SUMS[:, :], start=True, stop=True))
            op("vector", ["P4", "XH"], ["XH"],
               lambda v, pr=pr: v.tensor_mul(n.XH[:, pr * 512:(pr + 1) * 512],
                                             n.XH[:, pr * 512:(pr + 1) * 512], P4[:, :]))
        for mc in range(6):
            def t1mm(t, mc=mc):
                for dc in range(8):
                    i = t.matmul(P6[:, :], n.W0[:, dc * 768 + mc * 128:dc * 768 + mc * 128 + 128],
                                 n.XH[:, dc * 512:(dc + 1) * 512], start=(dc == 0), stop=(dc == 7))
                return i
            op("tensor", ["W0", "XH"], ["P6"], t1mm)
            op("vector", ["P6"], ["KD"],
               lambda v, mc=mc: v.tensor_copy(n.KD[:, mc * 512:(mc + 1) * 512], P6[:, :]))
        for fc in range(8):
            def aomm(t, fc=fc):
                for kc in range(6):
                    i = t.matmul(P7[:, :], n.W1[:, kc * 1024 + fc * 128:kc * 1024 + fc * 128 + 128],
                                 n.KD[:, kc * 512:(kc + 1) * 512], start=(kc == 0), stop=(kc == 5))
                return i
            op("tensor", ["W1", "KD"], ["P7"], aomm)
            op("vector", ["P7", "OUTB", "OT"], ["OT"],
               lambda v, fc=fc, t0=t0: v.scalar_tensor_tensor(
                   n.OT[:, fc * T + t0:fc * T + t0 + 512], P7[:, :], n.OUTB[:, fc:fc + 1],
                   n.OT[:, fc * T + t0:fc * T + t0 + 512], ALU.add, ALU.add))

    n.nc.leave_named_scope("ab_outp", _sid, False)
    _sid = n.nc.enter_named_scope("ab_mlp", False)[0]
    for eng in ("scalar", "gpsimd"):
        k.wait_all(eng, ["QS", "VTOK"])
    ln_full(n, 1e-5, srct=n.OT, srcn="OT")
    op("gpsimd", ["bfc1Ud"], ["W0"],
       lambda g: g.dma_start(out=n.W0[:, 0:4096].rearrange("p (c f) -> p c f", c=8),
                             in_=n.bfc1U.rearrange("(c p) f -> c p f", c=8).transpose([1, 0, 2])),
       chan=k.chan("w0"))
    op("gpsimd", ["bfc2Vd"], ["W0"],
       lambda g: g.dma_start(out=n.W0[:, 4096:8192].rearrange("p (c f) -> p c f", c=4),
                             in_=n.bfc2V.rearrange("(c p) f -> c p f", c=4).transpose([1, 0, 2])),
       chan=k.chan("w0b"))
    op("gpsimd", ["bfc1Vd"], ["W1"],
       lambda g: g.dma_start(
           out=n.W1[:, 0:8192].rearrange("p (c f) -> p c f", c=4),
           in_=n.bfc1V[:, 0:2048].rearrange("(c p) f -> c p f", c=4).transpose([1, 0, 2])),
       chan=k.chan("w1"))
    op("gpsimd", ["bfc1Vd"], ["WM"],
       lambda g: g.dma_start(
           out=n.WM[:, 0:8192].rearrange("p (c f) -> p c f", c=4),
           in_=n.bfc1V[:, 2048:4096].rearrange("(c p) f -> c p f", c=4).transpose([1, 0, 2])),
       chan=k.chan("wm"))
    for j in range(4):
        t0 = j * 512
        ln_apply(n, t0, srct=n.OT, srcn="OT", gt=n.LN2G, gn="LN2G", bt=n.LN2B, bn="LN2B")
        for mc in range(4):
            def m1(t, mc=mc):
                for dc in range(8):
                    i = t.matmul(P0[:, :], n.W0[:, dc * 512 + mc * 128:dc * 512 + mc * 128 + 128],
                                 n.XH[:, dc * 512:(dc + 1) * 512], start=(dc == 0), stop=(dc == 7))
                return i
            op("tensor", ["W0", "XH"], ["P0"], m1)
            op("vector", ["P0"], ["KD"],
               lambda v, mc=mc: v.tensor_copy(n.KD[:, mc * 512:(mc + 1) * 512], P0[:, :]))
        ACC = [(P4, "P4"), (P6, "P6"), (P7, "P7"), (P2, "P2")]

        def emit_m2(fc):
            WV, WVN = (n.W1, "W1") if fc < 16 else (n.WM, "WM")
            fcl = fc % 16
            bk2, bk2n = (P1, "P1") if fc % 2 == 0 else (P3, "P3")
            fun = "FU%d" % (fc % 8)
            fsl = n.VTOK[:, (fc % 8) * 512:(fc % 8) * 512 + 512]
            op("sync", ["fc2Ub"], [fun],
               lambda g, fc=fc, fsl=fsl: g.dma_start(
                   out=fsl, in_=n.fc2Ub.ap()[fc * 128:(fc + 1) * 128, :]),
               chan=k.chan("fu"))

            def m2(t, fcl=fcl, WV=WV, bk2=bk2):
                for kc in range(4):
                    i = t.matmul(bk2[:, :], WV[:, kc * 2048 + fcl * 128:kc * 2048 + fcl * 128 + 128],
                                 n.KD[:, kc * 512:(kc + 1) * 512], start=(kc == 0), stop=(kc == 3))
                return i
            op("tensor", [WVN, "KD"], [bk2n], m2)

        emit_m2(0)
        for fc in range(32):
            if fc + 1 < 32:
                emit_m2(fc + 1)
            bk2, bk2n = (P1, "P1") if fc % 2 == 0 else (P3, "P3")
            fun = "FU%d" % (fc % 8)
            fsl = n.VTOK[:, (fc % 8) * 512:(fc % 8) * 512 + 512]
            qgn = "QSg%d" % (fc % 8)
            qsl = n.QS[:, (fc % 8) * 512:(fc % 8) * 512 + 512]
            op("scalar", [bk2n, "FC1B"], [qgn],
               lambda s, fc=fc, qsl=qsl, bk2=bk2: s.activation(out=qsl, in_=bk2[:, :],
                                                               func=AF.Gelu_apprx_tanh,
                                                               bias=n.FC1B[:, fc:fc + 1]))
            for mc in range(4):
                bk, bkn = ACC[mc]

                def m3(t, fc=fc, mc=mc, bk=bk, fsl=fsl, qsl=qsl):
                    return t.matmul(bk[:, :], fsl[:, mc * 128:(mc + 1) * 128], qsl,
                                    start=(fc == 0), stop=(fc == 31))
                op("tensor", [fun, qgn], [bkn], m3)
        for mc in range(4):
            bk, bkn = ACC[mc]
            op("vector", [bkn], ["KD"],
               lambda v, mc=mc, bk=bk: v.tensor_copy(n.KD[:, mc * 512:(mc + 1) * 512], bk[:, :]))
        for fc in range(8):
            def m4(t, fc=fc):
                for kc in range(4):
                    i = t.matmul(P3[:, :], n.W0[:, 4096 + kc * 1024 + fc * 128:4096 + kc * 1024 + fc * 128 + 128],
                                 n.KD[:, kc * 512:(kc + 1) * 512], start=(kc == 0), stop=(kc == 3))
                return i
            op("tensor", ["W0", "KD"], ["P3"], m4)
            r = fc % 2
            XT = n.XT32A if r == 0 else n.XT32B
            XO = n.XO32A if r == 0 else n.XO32B
            xtn = "XT32%s" % ("A" if r == 0 else "B")
            xon = "XO32%s" % ("A" if r == 0 else "B")
            op("sync", ["xfp"], [xtn],
               lambda s, fc=fc, t0=t0, XT=XT: s.dma_start(
                   out=XT[:], in_=n.xfp.ap()[fc * 128:(fc + 1) * 128, t0:t0 + 512]),
               chan=k.chan("xt%d" % r))
            op("vector", ["P3", "FC2B", "OT", xtn], [xon],
               lambda v, fc=fc, t0=t0, XO=XO: v.scalar_tensor_tensor(
                   XO[:], n.P3[:, :], n.FC2B[:, fc:fc + 1],
                   n.OT[:, fc * T + t0:fc * T + t0 + 512], ALU.add, ALU.add))
            op("vector", [xon, xtn], [xon],
               lambda v, XO=XO, XT=XT: v.tensor_add(XO[:], XO[:], XT[:]))
            op("vector", [xon], ["XBF"],
               lambda v, fc=fc, t0=t0, XO=XO: v.tensor_copy(
                   n.XBF[:, fc * T + t0:fc * T + t0 + 512], XO[:]))
            op("sync", [xon], [dstname],
               lambda s, fc=fc, t0=t0, XO=XO: s.dma_start(
                   out=dst[fc * 128:(fc + 1) * 128, t0:t0 + 512], in_=XO[:]),
               chan=k.chan("xo%d" % r))

    slotnames = (["QSg%d" % i for i in range(8)] + ["FU%d" % i for i in range(8)] +
                 ["WM", "KDe0", "KDe1", "KDe2", "KDe3", "QTa", "QTb", "TMPc0", "TMPc1"])
    for eng in ("scalar", "vector", "tensor", "gpsimd", "sync"):
        k.wait_all(eng, slotnames)
    n.nc.leave_named_scope("ab_mlp", _sid, False)


def emit_debug(n, dbg_names):
    k, op = n.k, n.k.op
    dbg_map = {
        "TPOS": (n.TPOS, "TPOS"), "CUM": (n.CUM, "CUM"), "EQ": (n.EQ, "EQ"), "EK": (n.EK, "EK"),
        "QS": (n.QS, "QS"), "KD": (n.KD, "KD"), "OT": (n.OT, "OT"),
        "SST": (n.SST, "SST"), "CLALL": (n.CLALL, "CLALL"), "DD": (n.DD, "DD"),
        "XBF": (n.XBF, "XBF"), "XH": (n.XH, "XH"), "AM": (n.AM, "AM"), "PST": (n.PST, "PST"),
        "ROWA": (n.ROWA, "ROWA"), "ROWB": (n.ROWB, "ROWB"),
        "W1": (n.W1, "W1"), "W0": (n.W0, "W0"), "VTOK": (n.VTOK, "VTOK"),
    }
    for nm in dbg_names:
        buf, res = dbg_map[nm]
        o = n.nc.dram_tensor("dbg_" + nm, list(buf.shape), buf.dtype, kind="ExternalOutput").ap()
        op("gpsimd", [res], ["dbg_" + nm],
           lambda g2, o=o, buf=buf: g2.dma_start(out=o, in_=buf[:]), chan=k.chan("dbg"))


def fin(n):
    k = n.k

    def _fin(g2):
        for cname in ("xo0", "xo1", "dbg", "we0", "we1"):
            if cname in k.chans:
                c = k.chans[cname]
                g2.wait_ge(c.h, c.n)
    k.block.gpsimd(_fin)


def host_inputs(inp, r):
    p, role = r // 2, r % 2
    d = {}
    x = np.asarray(inp["L0_attn_a_x"])
    d["x"] = np.ascontiguousarray(x[p].T[:, role * T:(role + 1) * T])
    d["rolef"] = np.full((128, 1), float(role), dtype=np.float32)
    for nm in "qkvo":
        d["aW" + nm] = np.ascontiguousarray(np.asarray(inp["L0_attn_a_W" + nm]).T)
    d["aWf"] = np.ascontiguousarray(np.asarray(inp["L0_attn_a_Wf"]).T)
    d["adelta"] = np.ascontiguousarray(np.asarray(inp["L0_attn_a_delta"]).reshape(16, 1))
    d["anw"] = np.ascontiguousarray(np.asarray(inp["L0_attn_a_norm_weight"]).reshape(8, 128).T)
    d["fwfcT"] = np.ascontiguousarray(np.asarray(inp["L0_ffn_w_fc"]).T)
    d["fwprojT"] = np.ascontiguousarray(np.asarray(inp["L0_ffn_w_proj"]).T)
    d["fuT"] = np.ascontiguousarray(np.asarray(inp["L0_ffn_u"]).T)
    d["fv"] = np.ascontiguousarray(np.asarray(inp["L0_ffn_v"]))
    d["ffb"] = np.ascontiguousarray(np.asarray(inp["L0_ffn_fast_b"]))
    d["fgate"] = np.asarray(inp["L0_ffn_gate"]).reshape(1, 1).astype(np.float32)

    d["rolei"] = np.full((1, 1), role, dtype=np.int32)
    rs = np.zeros((128, 1), np.float32)
    rs[48, 0] = 1.0
    rs[112, 0] = 1.0
    d["rowsel"] = rs
    col8 = lambda a: np.ascontiguousarray(np.asarray(a).reshape(8, 128).T)
    d["bln1g"] = col8(inp["L0_attn_b_ln1_g"])
    d["bln1b"] = col8(inp["L0_attn_b_ln1_b"])
    d["bln2g"] = col8(inp["L0_attn_b_ln2_g"])
    d["bln2b"] = col8(inp["L0_attn_b_ln2_b"])

    def packU(U):
        U = np.asarray(U)
        out = np.zeros((1024, 1024), np.float32)
        for fc in range(8):
            out[:, fc * 128 + 0:fc * 128 + 48] = U[:, 2 * fc, :]
            out[:, fc * 128 + 64:fc * 128 + 112] = U[:, 2 * fc + 1, :]
        return out

    def packVb(V, b, sums_col=False):
        V = np.asarray(V)
        b = np.asarray(b)
        w = 65 if sums_col else 64
        out = np.zeros((128, 8 * w), np.float32)
        for pr in range(8):
            out[0:48, pr * w:pr * w + 64] = V[2 * pr]
            out[48, pr * w:pr * w + 64] = b[2 * pr]
            out[64:112, pr * w:pr * w + 64] = V[2 * pr + 1]
            out[112, pr * w:pr * w + 64] = b[2 * pr + 1]
            if sums_col:
                out[48, pr * w + 64] = 1.0
                out[112, pr * w + 64] = 1.0
        return out

    d["bqU"] = packU(inp["L0_attn_b_q_U"])
    d["bkU"] = packU(inp["L0_attn_b_k_U"])
    d["bvU"] = packU(inp["L0_attn_b_v_U"])
    d["bqVb"] = packVb(inp["L0_attn_b_q_V"], inp["L0_attn_b_q_b"])
    d["bkVb"] = packVb(inp["L0_attn_b_k_V"], inp["L0_attn_b_k_b"])
    d["bvVb"] = packVb(inp["L0_attn_b_v_V"], inp["L0_attn_b_v_b"], sums_col=True)
    d["boutU"] = np.ascontiguousarray(np.asarray(inp["L0_attn_b_out_U"], np.float32))
    d["boutV"] = np.ascontiguousarray(np.asarray(inp["L0_attn_b_out_V"], np.float32))
    d["boutb"] = col8(inp["L0_attn_b_out_b"])
    d["bfc1U"] = np.ascontiguousarray(np.asarray(inp["L0_attn_b_fc1_U"], np.float32))
    d["bfc1V"] = np.ascontiguousarray(np.asarray(inp["L0_attn_b_fc1_V"], np.float32))
    d["bfc1b"] = np.ascontiguousarray(np.asarray(inp["L0_attn_b_fc1_b"]).reshape(32, 128).T)
    d["bfc2U"] = np.ascontiguousarray(np.asarray(inp["L0_attn_b_fc2_U"], np.float32))
    d["bfc2V"] = np.ascontiguousarray(np.asarray(inp["L0_attn_b_fc2_V"], np.float32))
    d["bfc2b"] = col8(inp["L0_attn_b_fc2_b"])
    return d




_CACHE = {}


def kernel(**inputs):
    if "nc" not in _CACHE:
        _CACHE["nc"] = build((), ("gla", "fprep", "ffn1", "attnb", "ffn2"))
    nc = _CACHE["nc"]
    ins = [host_inputs(inputs, r) for r in range(8)]
    res = run_bass_kernel_spmd(nc, ins, core_ids=list(range(8)))
    out = np.empty((4, 4096, 1024), np.float32)
    for r in range(8):
        p, role = r // 2, r % 2
        out[p, role * T:(role + 1) * T, :] = np.asarray(res.results[r]["xout"]).T
    return out
